# version 17
# baseline (speedup 1.0000x reference)
from contextlib import ExitStack

import numpy as np
import ml_dtypes

import concourse.bass as bass
import concourse.mybir as mybir
from concourse.bass_utils import run_bass_kernel_spmd

F32 = mybir.dt.float32
BF16 = mybir.dt.bfloat16
U8 = mybir.dt.uint8
ALU = mybir.AluOpType
AFT = mybir.ActivationFunctionType

NCORES = 8
D = 2048
T = 1024
KC = 16
JC = 64
MODW = 1536
LN2 = float(np.log(np.float64(2.0)))
EPS = 1e-5
R2048 = float(1.0 / np.sqrt(np.float64(2048.0)))
R8192 = float(1.0 / np.sqrt(np.float64(8192.0)))
TINY = float(np.finfo(np.float32).tiny)


class Ctr:
    def __init__(self, sem):
        self.sem = sem
        self.v = 0

    def inc(self, n=1):
        self.v += n
        return self.v


def _fwht_schedule(n_chunks):
    stages = []
    s = 1
    while s < n_chunks:
        stages.append(s)
        s *= 2
    level = [0] * n_chunks
    pending = []
    for st_i, s in enumerate(stages):
        for a in range(n_chunks):
            if not (a & s):
                pending.append((st_i, a, a | s))
    events = []
    ready = [False] * n_chunks
    for c in range(n_chunks):
        ready[c] = True
        events.append(("ready", c))
        progress = True
        while progress:
            progress = False
            for op in list(pending):
                st_i, a, b = op
                if ready[a] and ready[b] and level[a] == st_i and level[b] == st_i:
                    events.append(("op", st_i, a, b))
                    level[a] = st_i + 1
                    level[b] = st_i + 1
                    pending.remove(op)
                    progress = True
    assert not pending
    return events


class _StopEmit(Exception):
    pass


def build_nc(debug=False, stop=None):
    nc = bass.Bass("TRN2", target_bir_lowering=False)

    d_x = nc.dram_tensor("x_fm", [D, T], F32, kind="ExternalInput").ap()
    d_wq = nc.dram_tensor("wqT", [8 * 128, 16 * 256], F32, kind="ExternalInput").ap()
    d_wv = nc.dram_tensor("wvT", [8 * 128, 16 * 256], F32, kind="ExternalInput").ap()
    d_wg = nc.dram_tensor("wgT", [8 * 128, 16 * 256], F32, kind="ExternalInput").ap()
    d_wo = nc.dram_tensor("woT", [8 * 128, 16 * 256], F32, kind="ExternalInput").ap()
    d_wup = nc.dram_tensor("wupT", [64 * 128, 16 * 128], F32, kind="ExternalInput").ap()
    d_wgt = nc.dram_tensor("wgtT", [64 * 128, 16 * 128], F32, kind="ExternalInput").ap()
    d_wdn = nc.dram_tensor("wdnT", [16 * 128, 64 * 128], F32, kind="ExternalInput").ap()
    d_wmod = nc.dram_tensor("wmod_sl", [12 * 128, 16 * 128], F32, kind="ExternalInput").ap()
    d_ccol = nc.dram_tensor("c_cols", [128, 16], F32, kind="ExternalInput").ap()
    d_beta = nc.dram_tensor("beta_row", [1, 16], F32, kind="ExternalInput").ap()
    d_gcol = nc.dram_tensor("g_cols", [128, 16], F32, kind="ExternalInput").ap()
    d_bcol = nc.dram_tensor("b_cols", [128, 16], F32, kind="ExternalInput").ap()
    d_sucol = nc.dram_tensor("su_cols", [128, 16], F32, kind="ExternalInput").ap()
    d_wsuu = nc.dram_tensor("wsuu_cols", [128, 64], F32, kind="ExternalInput").ap()
    d_wsug = nc.dram_tensor("wsug_cols", [128, 64], F32, kind="ExternalInput").ap()
    d_sudp = nc.dram_tensor("sudp_cols", [128, 64], F32, kind="ExternalInput").ap()
    d_wsdp = nc.dram_tensor("wsdp_cols", [128, 16], F32, kind="ExternalInput").ap()
    d_h128 = nc.dram_tensor("h128", [128, 128], BF16, kind="ExternalInput").ap()
    d_idbf = nc.dram_tensor("ident_bf", [128, 128], BF16, kind="ExternalInput").ap()
    d_idf32 = nc.dram_tensor("ident_f32", [128, 128], F32, kind="ExternalInput").ap()
    d_wexp = nc.dram_tensor("wexp_row", [8, 16], F32, kind="ExternalInput").ap()
    d_wmsk = nc.dram_tensor("wmask_row", [8, 16], F32, kind="ExternalInput").ap()
    d_out = nc.dram_tensor("OUT", [D, T], F32, kind="ExternalOutput").ap()

    d_x1 = nc.dram_tensor("x1_scratch", [D, T], F32).ap()
    d_cc1i = nc.dram_tensor("cc1_in", [12, 128], F32)
    d_cc1o = nc.dram_tensor("cc1_out", [96, 128], F32)
    d_cc2i = nc.dram_tensor("cc2_in", [16, 128], F32)
    d_cc2o = nc.dram_tensor("cc2_out", [8 * 16, 128], F32)

    dbg = {}
    if debug or stop:
        dbg["A"] = nc.dram_tensor("DBG_A", [8192, T], BF16, kind="ExternalOutput").ap()
        dbg["B"] = nc.dram_tensor("DBG_B", [D, T], BF16, kind="ExternalOutput").ap()
        dbg["C"] = nc.dram_tensor("DBG_C", [D, T], BF16, kind="ExternalOutput").ap()
        dbg["S"] = nc.dram_tensor("DBG_S", [128, 256], F32, kind="ExternalOutput").ap()
        dbg["R"] = nc.dram_tensor("DBG_R", [1, 2048], F32, kind="ExternalOutput").ap()

    arena = nc.alloc_sbuf_tensor("arena", [128, 160 * 1024], U8)
    abf = arena.bitcast(BF16)
    af32 = arena.bitcast(F32)
    RA, RB, RC, RD, RE = 0, 16384, 32768, 49152, 65536
    xbf = lambda c: abf[:, RA + c * T: RA + (c + 1) * T]
    qret = lambda c: abf[:, RA + c * T: RA + (c + 1) * T]
    x1bf = lambda c: abf[:, RA + c * T: RA + (c + 1) * T]
    out2 = lambda c: abf[:, RB + c * T: RB + (c + 1) * T]
    wmod_sb = af32[:, 16384: 16384 + KC * MODW]
    ret = lambda c: abf[:, RC + c * T: RC + (c + 1) * T]
    sg = lambda c: abf[:, RD + c * T: RD + (c + 1) * T]
    h_sl = lambda c: abf[:, RE + c * T: RE + (c + 1) * T]
    z_raw = lambda i: abf[:, RE + i * T: RE + (i + 1) * T]
    u_raw = lambda i: abf[:, RA + i * T: RA + (i + 1) * T]
    dwslot = lambda s: abf[:, RE + s * 8192: RE + (s + 1) * 8192]

    un = nc.alloc_sbuf_tensor("units", [128, 4 * 1024], F32)
    U0, U1 = un[:, 0:1024], un[:, 1024:2048]
    U2, U3 = un[:, 2048:3072], un[:, 3072:4096]
    un_bf = un.bitcast(BF16)
    SPARE1 = un_bf[:, 0:1024]
    SPARE2 = un_bf[:, 1024:2048]
    SQ0 = un_bf[:, 4096:5120]
    SQ1 = un_bf[:, 6144:7168]
    H2T0 = un_bf[:, 5120:6144]
    H2T1 = un_bf[:, 7168:8192]

    rowmm = nc.alloc_sbuf_tensor("rowmm", [128, 2048], F32)
    row_rsq = rowmm[0:1, 0:1024]
    row_mu = rowmm[0:1, 1024:2048]
    sacc = rowmm[0:1, 0:2048]
    w8_sb = nc.alloc_sbuf_tensor("w8_sb", [8, 16], F32)
    lnl8_sb = nc.alloc_sbuf_tensor("lnl8_sb", [8, 16], F32)
    ones8 = nc.alloc_sbuf_tensor("ones8", [8, 1], F32)
    piece0 = un[0:1, 0:2048]
    piece1 = un[0:1, 2048:4096]
    prow0 = un[0:1, 2048:3072]
    prow1 = un[0:1, 3072:4096]
    row_var = un[0:1, 3072:4096]

    wbuf = nc.alloc_sbuf_tensor("wbuf", [128, 16 * 1024], U8)
    wb_bf = wbuf.bitcast(BF16)
    awslot = lambda s: wb_bf[:, s * 4096:(s + 1) * 4096]
    fwslot = lambda s: wb_bf[:, s * 2048:(s + 1) * 2048]

    h128_sb = nc.alloc_sbuf_tensor("h128_sb", [128, 128], BF16)
    idbf_sb = nc.alloc_sbuf_tensor("idbf_sb", [128, 128], BF16)
    idf32_sb = nc.alloc_sbuf_tensor("idf32_sb", [128, 128], F32)
    ccol_sb = nc.alloc_sbuf_tensor("ccol_sb", [128, 16], F32)
    cbf_sb = ccol_sb
    beta_sb = nc.alloc_sbuf_tensor("beta_sb", [1, 16], F32)
    lam_row = nc.alloc_sbuf_tensor("lam_row", [1, 16], F32)
    lnlam_row = nc.alloc_sbuf_tensor("lnlam_row", [1, 16], F32)
    lam_cols = nc.alloc_sbuf_tensor("lam_cols", [128, 16], F32)
    gcol_sb = nc.alloc_sbuf_tensor("gcol_sb", [128, 16], F32)
    bcol_sb = nc.alloc_sbuf_tensor("bcol_sb", [128, 16], F32)
    sucol_sb = nc.alloc_sbuf_tensor("sucol_sb", [128, 16], F32)
    wsuu_sb = nc.alloc_sbuf_tensor("wsuu_sb", [128, 64], F32)
    wsug_sb = nc.alloc_sbuf_tensor("wsug_sb", [128, 64], F32)
    sudp_sb = nc.alloc_sbuf_tensor("sudp_sb", [128, 64], F32)
    wsdp_sb = nc.alloc_sbuf_tensor("wsdp_sb", [128, 16], F32)
    usc_sb = nc.alloc_sbuf_tensor("usc_sb", [128, 64], F32)
    gsc_sb = nc.alloc_sbuf_tensor("gsc_sb", [128, 64], F32)
    gwd_sb = nc.alloc_sbuf_tensor("gwd_sb", [128, 16], F32)
    modc_sb = nc.alloc_sbuf_tensor("modc_sb", [128, 96], F32)
    modp_sb = nc.alloc_sbuf_tensor("modp_sb", [128, 96], F32)
    scr16 = un[:, 2048:2064]
    scr16b = un[:, 3072:3088]
    ones_col = nc.alloc_sbuf_tensor("ones_col", [128, 1], BF16)
    ones_row = nc.alloc_sbuf_tensor("ones_row", [1, 128], F32)
    wexp_sb = rowmm[0:8, 2304:2320].transpose([0, 1]) if False else None
    wexp_sb = nc.alloc_sbuf_tensor("wexp_sb", [8, 16], F32)
    wmsk_sb = nc.alloc_sbuf_tensor("wmsk_sb", [8, 16], F32)
    modsl_sb = nc.alloc_sbuf_tensor("modsl_sb", [128, 12], F32)
    modtr_sb = nc.alloc_sbuf_tensor("modtr_sb", [12, 128], F32)
    cc1_sb = nc.alloc_sbuf_tensor("cc1_sb", [96, 128], F32)
    lbuf_sb = nc.alloc_sbuf_tensor("lbuf_sb", [128, 16], BF16)
    ltr_sb = nc.alloc_sbuf_tensor("ltr_sb", [16, 128], F32)

    sa_c = lambda c: modp_sb[:, 0 + c: 1 + c]
    ca1_c = lambda c: modp_sb[:, 16 + c: 17 + c]
    gaf_c = lambda c: modp_sb[:, 32 + c: 33 + c]
    scs_c = lambda c: modp_sb[:, 48 + c: 49 + c]
    sbs_c = lambda c: modp_sb[:, 64 + c: 65 + c]

    es = ExitStack()
    P0 = es.enter_context(nc.psum_tensor("P0", [128, 1024], F32))
    P1 = es.enter_context(nc.psum_tensor("P1", [128, 1024], F32))
    P2 = es.enter_context(nc.psum_tensor("P2", [128, 1024], F32))
    P3 = es.enter_context(nc.psum_tensor("P3", [128, 1024], F32))
    PP = [P0, P1, P2, P3]
    row_t1 = P1[0:1, 0:1024]

    sems = {}
    for i in range(12):
        sems[f"sWM{i}"] = Ctr(es.enter_context(nc.semaphore(f"sWM{i}")))
    sWMs = [sems[f"sWM{i}"] for i in range(12)]
    for nm in ["sMisc", "sU0", "sU1", "sSpill", "sOut", "sOutB", "sWM", "sCC", "sPC", "sPCB", "sGP",
               "sWA0", "sWA1", "sWU0", "sWU1", "sWG0", "sWG1", "sWD0", "sWD1",
               "sT", "sV", "sS", "sX2"]:
        sems[nm] = Ctr(es.enter_context(nc.semaphore(nm)))
    sMisc, sU0c, sU1c = sems["sMisc"], sems["sU0"], sems["sU1"]
    sSpill, sOut, sWM, sCC = sems["sSpill"], sems["sOut"], sems["sWM"], sems["sCC"]
    sPC = [sems["sPC"], sems["sPCB"]]
    sOutP = [sems["sOut"], sems["sOutB"]]
    sWA = [sems["sWA0"], sems["sWA1"]]
    sWU = [sems["sWU0"], sems["sWU1"]]
    sWG = [sems["sWG0"], sems["sWG1"]]
    sWD = [sems["sWD0"], sems["sWD1"]]
    sT, sV, sS, sX2 = sems["sT"], sems["sV"], sems["sS"], sems["sX2"]

    block = es.enter_context(nc.Block())

    M = {}

    def mark(ctr, name):
        M[name] = (ctr, ctr.v)

    def wait(eng, name):
        ctr, v = M[name]
        if v > 0:
            eng.wait_ge(ctr.sem, v)

    TS, VS, SS, GS, YS = block.tensor, block.vector, block.scalar, block.gpsimd, block.sync

    sDbg = Ctr(es.enter_context(nc.semaphore("sDbg")))

    def chk(name, waits, dumps):
        if stop != name:
            return

        def _(y):
            for w in waits:
                wait(y, w)
            for (dst, src) in dumps:
                y.dma_start(out=dst, in_=src).then_inc(sDbg.sem, 16)
                sDbg.inc(16)
            y.wait_ge(sDbg.sem, sDbg.v)
        YS(_)
        raise _StopEmit()

    def _phases():
        def attn_wblock(g, dsrc, widx, b):
            slot = b % 2
            if b >= 2:
                wait(g, f"aw{widx}_b{b-2}")
            elif widx > 0:
                wait(g, f"aw{widx-1}_b{6 + b}")
            g.dma_start(out=awslot(slot),
                        in_=dsrc[b * 128:(b + 1) * 128, :]).then_inc(sWA[slot].sem, 16)
            sWA[slot].inc(16)
            mark(sWA[slot], f"aw{widx}_f{b}")

        def mm_chunk(t, dst, wtile_fn, rhs_fn):
            last = None
            for ci in range(KC):
                lhs = wtile_fn(ci)
                rhs = rhs_fn(ci)
                for hf in range(2):
                    last = t.matmul(dst[:, hf * 512:(hf + 1) * 512], lhs,
                                    rhs[:, hf * 512:(hf + 1) * 512],
                                    start=(ci == 0), stop=(ci == KC - 1))
            return last

        def stat_mms(t, src_bf, sq_bf, first, last, sq_dst=None):
            sq_dst = P2[0:1, 0:1024] if sq_dst is None else sq_dst
            li = None
            for hf in range(2):
                t.matmul(P3[0:1, hf * 512:(hf + 1) * 512], ones_col[:],
                         src_bf[:, hf * 512:(hf + 1) * 512], start=first, stop=last)
            for hf in range(2):
                li = t.matmul(sq_dst[0:1, hf * 512:(hf + 1) * 512], ones_col[:],
                              sq_bf[:, hf * 512:(hf + 1) * 512], start=first, stop=last)
            return li

        def rows_and_bcast(phase, pb_rsq, pb_mu, var_src=None):
            def _(v):
                wait(v, f"{phase}_stats")
                v.tensor_scalar_mul(row_mu[:], P3[0:1, 0:1024], 1.0 / D)
                v.tensor_scalar_mul(row_var[:], (P2[0:1, 0:1024] if var_src is None
                                                 else var_src), 1.0 / D)
                v.drain()
                v.tensor_mul(row_t1[:], row_mu[:], row_mu[:])
                v.drain()
                v.tensor_sub(row_var[:], row_var[:], row_t1[:])
                v.drain()
                v.tensor_scalar(row_var[:], row_var[:], EPS, None, ALU.add)
                v.drain().then_inc(sV.sem, 1)
                sV.inc()
                mark(sV, f"{phase}_var")
            VS(_)

            def _(s):
                wait(s, f"{phase}_var")
                s.activation(row_t1[:], row_var[:], AFT.Ln)
                s.drain()
                s.activation(row_rsq[:], row_t1[:], AFT.Exp, scale=-0.5).then_inc(sS.sem, 1)
                sS.inc()
                mark(sS, f"{phase}_rows")
            SS(_)

            def _(t):
                wait(t, f"{phase}_rows")
                li = None
                for hf in range(2):
                    t.matmul(pb_rsq[:, hf * 512:(hf + 1) * 512], ones_row[:],
                             row_rsq[:, hf * 512:(hf + 1) * 512], start=True, stop=True)
                    li = t.matmul(pb_mu[:, hf * 512:(hf + 1) * 512], ones_row[:],
                                  row_mu[:, hf * 512:(hf + 1) * 512], start=True, stop=True)
                li.then_inc(sT.sem, 1)
                sT.inc()
                mark(sT, f"{phase}_bcast")
            TS(_)




        def _(y):
            for (dst, src) in [
                (h128_sb[:], d_h128), (idbf_sb[:], d_idbf), (idf32_sb[:], d_idf32),
                (ccol_sb[:], d_ccol),
                (beta_sb[:], d_beta), (gcol_sb[:], d_gcol), (bcol_sb[:], d_bcol),
                (sucol_sb[:], d_sucol), (wsuu_sb[:], d_wsuu), (wsug_sb[:], d_wsug),
                (sudp_sb[:], d_sudp), (wsdp_sb[:], d_wsdp),
                (wexp_sb[:], d_wexp), (wmsk_sb[:], d_wmsk),
            ]:
                y.dma_start(out=dst, in_=src).then_inc(sMisc.sem, 16)
                sMisc.inc(16)
            mark(sMisc, "consts")
        YS(_)

        def _(x):
            for m in range(12):
                x.dma_start(out=wmod_sb[:, m * 2048:(m + 1) * 2048],
                            in_=d_wmod[m * 128:(m + 1) * 128, :]).then_inc(sWMs[m].sem, 16)
                sWMs[m].inc(16)
                mark(sWMs[m], f"wmod{m}")
        SS(_)

        def _(v):
            wait(v, "consts")
            v.memset(ones_col[:], 1.0)
            v.memset(ones_row[:], 1.0)
            v.memset(ones8[:], 1.0)
            v.tensor_scalar_mul(lam_row[:], beta_sb[:], -LN2).then_inc(sV.sem, 1)
            sV.inc()
            mark(sV, "lam_pre")
        VS(_)

        def _(s):
            wait(s, "lam_pre")
            s.activation(lam_row[:], lam_row[:], AFT.Exp).then_inc(sS.sem, 1)
            sS.inc()
            mark(sS, "exp_lam")
            s.activation(cbf_sb[:], ccol_sb[:], AFT.Silu).then_inc(sS.sem, 1)
            sS.inc()
            mark(sS, "siluc")
        SS(_)

        def _(v):
            wait(v, "exp_lam")
            v.tensor_scalar(lam_row[:], lam_row[:], -1.0, 1.0, ALU.mult, ALU.add)
            v.drain()
            v.tensor_scalar(lam_row[:], lam_row[:], TINY, 1.0, ALU.max, ALU.min)
            v.drain()
            v.tensor_copy(lnlam_row[:], lam_row[:])
            v.drain().then_inc(sV.sem, 1)
            sV.inc()
            mark(sV, "lam_done")
        VS(_)

        def _(s):
            wait(s, "lam_done")
            s.activation(lnlam_row[:], lnlam_row[:], AFT.Ln).then_inc(sS.sem, 1)
            sS.inc()
            mark(sS, "lnlam")
        SS(_)

        def _(t):
            wait(t, "lam_done")
            t.matmul(P2[0:128, 0:16], ones_row[:], lam_row[:], start=True,
                     stop=True)
            wait(t, "lnlam")
            t.matmul(P2[0:8, 16:32], ones_row[:, 0:8], lnlam_row[:], start=True,
                     stop=True).then_inc(sT.sem, 1)
            sT.inc()
            mark(sT, "lamb")
        TS(_)

        def _(s):
            wait(s, "lamb")
            s.copy(lam_cols[:], P2[0:128, 0:16])
            s.copy(lnl8_sb[:], P2[0:8, 16:32]).then_inc(sS.sem, 1)
            sS.inc()
            mark(sS, "lamcols")
        SS(_)

        MODPP = [P0, P1, P3]
        for m in range(12):
            def _(t, m=m):
                if m == 0:
                    wait(t, "siluc")
                wait(t, f"wmod{m}")
                if m >= 3:
                    wait(t, f"modev{m-3}")
                pcol = MODPP[m % 3][:, 0:1]
                li = None
                for ci in range(KC):
                    li = t.matmul(pcol,
                                  wmod_sb[:, m * 2048 + ci * 128: m * 2048 + (ci + 1) * 128],
                                  cbf_sb[:, ci:ci + 1],
                                  start=(ci == 0), stop=(ci == KC - 1))
                li.then_inc(sT.sem, 1)
                sT.inc()
                mark(sT, f"modmm{m}")
            TS(_)

            def _(s, m=m):
                wait(s, f"modmm{m}")
                s.copy(modsl_sb[:, m:m + 1], MODPP[m % 3][:, 0:1]).then_inc(sS.sem, 1)
                sS.inc()
                mark(sS, f"modev{m}")
            SS(_)

        def _(t):
            wait(t, "modev11")
            t.transpose(P2[0:12, 0:128], modsl_sb[:], idf32_sb[:]).then_inc(sT.sem, 1)
            sT.inc()
            mark(sT, "modtrT")
        TS(_)

        def _(s):
            wait(s, "modtrT")
            s.copy(modtr_sb[:], P2[0:12, 0:128]).then_inc(sS.sem, 1)
            sS.inc()
            mark(sS, "modtr")
            s.drain()
            s.dma_start(out=d_cc1i[:, :], in_=modtr_sb[:]).then_inc(sX2.sem, 16)
            sX2.inc(16)
            mark(sX2, "cc1in")
        SS(_)

        def _(g):
            wait(g, "cc1in")
            g.collective_compute("AllGather", ALU.bypass,
                                 replica_groups=[list(range(NCORES))],
                                 ins=[d_cc1i.ap().opt()], outs=[d_cc1o.ap().opt()],
                                 ).then_inc(sCC.sem, 1)
            sCC.inc(1)
            mark(sCC, "cc1")
        GS(_)

        for c in range(KC):
            def _(y, c=c):
                sl = [sU0c, sU1c][c % 2]
                if c >= 2:
                    wait(y, f"xcast{c-2}")
                y.dma_start(out=[U0, U1][c % 2],
                            in_=d_x[c * 128:(c + 1) * 128, :]).then_inc(sl.sem, 16)
                sl.inc(16)
                mark(sl, f"xload{c}")
            YS(_)

            def _(s, c=c):
                wait(s, f"xload{c}")
                s.copy(xbf(c), [U0, U1][c % 2]).then_inc(sS.sem, 1)
                sS.inc()
                mark(sS, f"xcast{c}")
            SS(_)

            def _(v, c=c):
                wait(v, f"xcast{c}")
                if c >= 2:
                    wait(v, f"xst{c-2}")
                sq = [SQ0, SQ1][c % 2]
                v.tensor_mul(sq, xbf(c), xbf(c)).then_inc(sV.sem, 1)
                sV.inc()
                mark(sV, f"xsq{c}")
            VS(_)

            def _(t, c=c):
                wait(t, f"xsq{c}")
                if c == 0:
                    wait(t, "modtr")
                stat_mms(t, xbf(c), [SQ0, SQ1][c % 2],
                         c == 0, c == KC - 1).then_inc(sT.sem, 1)
                sT.inc()
                mark(sT, f"xst{c}")
                if c == KC - 1:
                    mark(sT, "ph1_stats")
            TS(_)

        rows_and_bcast("ph1", P2, P3)


        def _(y):
            wait(y, "cc1")
            y.dma_start(out=cc1_sb[:], in_=d_cc1o.ap()).then_inc(sX2.sem, 16)
            sX2.inc(16)
            mark(sX2, "cc1sb")
        YS(_)

        def _(t):
            wait(t, "cc1sb")
            t.transpose(P0[0:128, 512:608], cc1_sb[:],
                        idf32_sb[0:96, 0:96]).then_inc(sT.sem, 1)
            sT.inc()
            mark(sT, "modcT")
        TS(_)

        def _(s):
            wait(s, "modcT")
            s.copy(modc_sb[:], P0[0:128, 512:608]).then_inc(sS.sem, 1)
            sS.inc()
            mark(sS, "modc")
        SS(_)

        def _(v):
            wait(v, "modc")
            v.tensor_copy(modp_sb[:, 0:16], modc_sb[:, 0:16])
            v.tensor_scalar(modp_sb[:, 16:32], modc_sb[:, 16:32], 1.0, None, ALU.add)
            for (src_g, dst) in [(2, 32), (5, 80)]:
                gac = modc_sb[:, src_g * 16:(src_g + 1) * 16]
                v.drain()
                v.tensor_single_scalar(scr16[:], gac, 0.0, ALU.is_ge)
                v.drain()
                v.tensor_scalar(scr16[:], scr16[:], 2.0, -1.0, ALU.mult, ALU.add)
                v.drain()
                v.tensor_mul(scr16b[:], gac, scr16[:])
                v.drain()
                v.tensor_scalar(scr16b[:], scr16b[:], 0.25, None, ALU.max)
                v.drain()
                v.tensor_mul(modp_sb[:, dst:dst + 16], scr16[:], scr16b[:])
            v.drain()
            v.tensor_scalar(scr16[:], modc_sb[:, 64:80], 1.0, None, ALU.add)
            v.drain()
            v.tensor_mul(modp_sb[:, 48:64], scr16[:], sucol_sb[:])
            v.tensor_mul(modp_sb[:, 64:80], modc_sb[:, 48:64], sucol_sb[:])
            v.tensor_mul(gwd_sb[:], modp_sb[:, 80:96], wsdp_sb[:])
            v.drain()
            v.tensor_scalar_mul(gwd_sb[:], gwd_sb[:], R8192)
            v.tensor_mul(usc_sb[:], wsuu_sb[:], sudp_sb[:])
            v.drain()
            v.tensor_scalar_mul(usc_sb[:], usc_sb[:], R2048)
            v.tensor_scalar_mul(gsc_sb[:], wsug_sb[:], R2048)
            v.drain().then_inc(sV.sem, 1)
            sV.inc()
            mark(sV, "mod_ready")
        VS(_)
        chk("mod", ["mod_ready"], [
            (dbg["S"][:, 0:96], modp_sb[:]),
            (dbg["S"][:, 96:112], lam_cols[:]),
            (dbg["S"][:, 112:128], gwd_sb[:]),
            (dbg["S"][:, 128:192], usc_sb[:]),
            (dbg["S"][:, 192:256], gsc_sb[:]),
        ] if stop == "mod" else [])

        def _(v):
            wait(v, "lamcols")
            wait(v, "consts")
            v.tensor_tensor(w8_sb[:], wexp_sb[:], lnl8_sb[:], ALU.mult
                            ).then_inc(sV.sem, 1)
            sV.inc()
            mark(sV, "w8log")
        VS(_)

        def _(s):
            wait(s, "w8log")
            s.activation(w8_sb[:], w8_sb[:], AFT.Exp).then_inc(sS.sem, 1)
            sS.inc()
            mark(sS, "w8exp")
        SS(_)

        def _(v):
            wait(v, "w8exp")
            v.tensor_mul(w8_sb[:], w8_sb[:], wmsk_sb[:])
            v.drain().then_inc(sV.sem, 1)
            sV.inc()
            mark(sV, "w8")
        VS(_)

        for c in range(KC):
            def _(v, c=c):
                if c == 0:
                    wait(v, "ph1_bcast")
                if c >= 2:
                    wait(v, f"h{c-2}")
                t1 = [U0, U1][c % 2]
                v.tensor_sub(t1, xbf(c), P3[:, 0:1024])
                v.drain()
                v.tensor_mul(t1, t1, P2[:, 0:1024]).then_inc(sV.sem, 1)
                sV.inc()
                mark(sV, f"xn{c}")
            VS(_)

            if c % 2 == 1:
                def _(s, c=c):
                    if c == 1:
                        wait(s, "mod_ready")
                    wait(s, f"xn{c}")
                    for cc2 in (c - 1, c):
                        s.activation(h_sl(cc2), [U0, U1][cc2 % 2], AFT.Identity,
                                     scale=ca1_c(cc2), bias=sa_c(cc2)
                                     ).then_inc(sS.sem, 1)
                        sS.inc()
                        mark(sS, f"h{cc2}")
                    if c == KC - 1:
                        mark(sS, "h_done")
                SS(_)

        chk("h", ["h_done"], [(dbg["B"][c * 128:(c + 1) * 128, :], h_sl(c))
                              for c in range(KC)] if stop == "h" else [])


        for c in range(KC):
            if c % 2 == 0:
                def _(g, c=c):
                    attn_wblock(g, d_wv, 0, c // 2)
                GS(_)

            def _(t, c=c):
                slot = (c // 2) % 2
                if c == 0:
                    wait(t, "h_done")
                wait(t, f"aw0_f{c//2}")
                if c >= 2:
                    wait(t, f"scan{c-2}")
                wtile = awslot(slot).rearrange("p (ci j) -> p ci j", ci=16)
                mm_chunk(t, PP[c % 2],
                         lambda ci: wtile[:, ci, (c % 2) * 128:(c % 2) * 128 + 128],
                         lambda ci: h_sl(ci)).then_inc(sT.sem, 1)
                sT.inc()
                mark(sT, f"vmm{c}")
                if c % 2 == 1:
                    mark(sT, f"aw0_b{c//2}")
            TS(_)

            def _(v, c=c):
                wait(v, f"vmm{c}")
                v.tensor_tensor_scan(ret(c), lam_cols[:, c:c + 1].broadcast_to([128, 1024]),
                                     PP[c % 2][:], 0.0, ALU.mult, ALU.add
                                     ).then_inc(sV.sem, 1)
                sV.inc()
                mark(sV, f"scan{c}")
            VS(_)

        def _(v):
            retv = abf[:, RC: RC + KC * T].rearrange("p (c t) -> p c t", c=KC)
            v.drain()
            v.tensor_copy(lbuf_sb[:], retv[:, :, 1023]).then_inc(sV.sem, 1)
            sV.inc()
            mark(sV, "lbuf")
        VS(_)
        chk("scan", ["lbuf"], [(dbg["B"][c * 128:(c + 1) * 128, :], ret(c))
                               for c in range(KC)] if stop == "scan" else [])

        for c in range(KC):
            if c % 2 == 0:
                def _(g, c=c):
                    attn_wblock(g, d_wg, 1, c // 2)
                GS(_)

            def _(t, c=c):
                slot = (c // 2) % 2
                wait(t, f"aw1_f{c//2}")
                if c >= 2:
                    wait(t, f"sgev{c-2}")
                else:
                    wait(t, f"scan{14 + c}")
                wtile = awslot(slot).rearrange("p (ci j) -> p ci j", ci=16)
                mm_chunk(t, PP[c % 2],
                         lambda ci: wtile[:, ci, (c % 2) * 128:(c % 2) * 128 + 128],
                         lambda ci: h_sl(ci)).then_inc(sT.sem, 1)
                sT.inc()
                mark(sT, f"gmm{c}")
                if c % 2 == 1:
                    mark(sT, f"aw1_b{c//2}")
            TS(_)

            def _(s, c=c):
                wait(s, f"gmm{c}")
                s.activation(sg(c), PP[c % 2][:], AFT.Silu).then_inc(sS.sem, 1)
                sS.inc()
                mark(sS, f"sgev{c}")
            SS(_)

            if c == 1:
                def _(t):
                    wait(t, "lbuf")
                    t.transpose(P2.bitcast(BF16)[0:16, 0:128], lbuf_sb[:],
                                idbf_sb[:]).then_inc(sT.sem, 1)
                    sT.inc()
                    mark(sT, "ltrT")
                TS(_)

                def _(s):
                    wait(s, "ltrT")
                    s.copy(ltr_sb[:], P2.bitcast(BF16)[0:16, 0:128]).then_inc(sS.sem, 1)
                    sS.inc()
                    mark(sS, "ltr")
                SS(_)

                def _(y):
                    wait(y, "ltr")
                    y.dma_start(out=d_cc2i[:, :], in_=ltr_sb[:]).then_inc(sX2.sem, 16)
                    sX2.inc(16)
                    mark(sX2, "cc2in")
                YS(_)

        sacc_bf = un_bf[0:1, 0:2048]
        rowmm_bf = rowmm.bitcast(BF16)
        prow_bf = [rowmm_bf[0:1, 0:1024], rowmm_bf[0:1, 1024:2048]]

        def _(g):
            wait(g, "cc2in")
            g.collective_compute("AllGather", ALU.bypass,
                                 replica_groups=[list(range(NCORES))],
                                 ins=[d_cc2i.ap().opt()], outs=[d_cc2o.ap().opt()],
                                 ).then_inc(sCC.sem, 1)
            sCC.inc(1)
            mark(sCC, "cc2")
            attn_wblock(g, d_wq, 2, 0)
            attn_wblock(g, d_wq, 2, 1)
        GS(_)

        L8 = af32[0:8, 0:2048]

        def _(y):
            wait(y, "cc2")
            y.dma_start(out=L8, in_=d_cc2o.ap()
                        .rearrange("(j c) p -> j (c p)", c=16)).then_inc(sX2.sem, 16)
            sX2.inc(16)
            mark(sX2, "L8")
        YS(_)

        def corrT_sec(c):
            def _(t, c=c):
                if c == 0:
                    wait(t, "sacc")
                wait(t, f"prow{c}")
                if c >= 1:
                    wait(t, f"corradd{c-1}")
                pr = prow_bf[c % 2]
                li = None
                for hf in range(2):
                    li = t.matmul(P2[:, hf * 512:(hf + 1) * 512],
                                  sacc_bf[:, c * 128:(c + 1) * 128],
                                  pr[:, hf * 512:(hf + 1) * 512], start=True, stop=True)
                li.then_inc(sT.sem, 1)
                sT.inc()
                mark(sT, f"corrT{c}")
            TS(_)

        def qst_sec(c):
            def _(t, c=c):
                wait(t, f"qsq{c}")
                stat_mms(t, qret(c), [SQ0, SQ1][c % 2],
                         c == 0, c == KC - 1, sq_dst=P3[32:33]).then_inc(sT.sem, 1)
                sT.inc()
                mark(sT, f"qst{c}")
                if c == KC - 1:
                    mark(sT, "ph5_stats")
            TS(_)

        def prow_sec(c):
            def _(v, c=c):
                if c >= 2:
                    wait(v, f"corrT{c-2}")
                else:
                    v.drain()
                pr = prow_bf[c % 2]
                lamb = lam_row[:, c:c + 1].broadcast_to([1, 1024])
                v.tensor_tensor_scan(pr, lamb, lamb, 1.0, ALU.mult, ALU.bypass
                                     ).then_inc(sV.sem, 1)
                sV.inc()
                mark(sV, f"prow{c}")
            VS(_)

        def corradd_sec(c):
            def _(v, c=c):
                wait(v, f"corrT{c}")
                v.drain()
                v.tensor_add(ret(c), ret(c), P2[:]).then_inc(sV.sem, 1)
                sV.inc()
                mark(sV, f"corradd{c}")
                if c == KC - 1:
                    mark(sV, "corr_done")
            VS(_)

        def qret_sec(c):
            def _(v, c=c):
                wait(v, f"qmm{c}")
                v.drain()
                v.tensor_mul(qret(c), PP[c % 2][:], ret(c)).then_inc(sV.sem, 1)
                sV.inc()
                mark(sV, f"qret{c}")
            VS(_)

        def qsq_sec(c):
            def _(v, c=c):
                if c >= 2:
                    wait(v, f"qst{c-2}")
                sq = [SQ0, SQ1][c % 2]
                v.drain()
                v.tensor_mul(sq, qret(c), qret(c)).then_inc(sV.sem, 1)
                sV.inc()
                mark(sV, f"qsq{c}")
            VS(_)

        def qmm_sec(c):
            def _(t, c=c):
                slot = (c // 2) % 2
                wait(t, f"aw2_f{c//2}")
                if c >= 2:
                    wait(t, f"qret{c-2}")
                else:
                    wait(t, f"sgev{14 + c}")
                wtile = awslot(slot).rearrange("p (ci j) -> p ci j", ci=16)
                mm_chunk(t, PP[c % 2],
                         lambda ci: wtile[:, ci, (c % 2) * 128:(c % 2) * 128 + 128],
                         lambda ci: h_sl(ci)).then_inc(sT.sem, 1)
                sT.inc()
                mark(sT, f"qmm{c}")
                if c % 2 == 1:
                    mark(sT, f"aw2_b{c//2}")
            TS(_)

        for cc in range(KC + 4):
            if cc % 2 == 0 and 4 <= cc < KC:
                def _(g, cc=cc):
                    attn_wblock(g, d_wq, 2, cc // 2)
                GS(_)

            if 1 <= cc <= KC:
                prow_sec(cc - 1)

            if cc == 1:
                def _(v):
                    wait(v, "L8")
                    wait(v, "w8")
                    v.tensor_tensor(
                        L8.rearrange("j (c p) -> j c p", c=16),
                        L8.rearrange("j (c p) -> j c p", c=16),
                        w8_sb[:].unsqueeze(2).broadcast_to([8, 16, 128]), ALU.mult)
                    v.drain().then_inc(sV.sem, 1)
                    sV.inc()
                    mark(sV, "L8s")
                VS(_)

            if cc == 2:
                def _(v):
                    wait(v, "saccT")
                    v.tensor_copy(sacc_bf[0:1, 0:1024], P2[0:1, 0:1024])
                    v.tensor_copy(sacc_bf[0:1, 1024:2048], P3[0:1, 0:1024])
                    v.drain().then_inc(sV.sem, 1)
                    sV.inc()
                    mark(sV, "sacc")
                VS(_)

            if 2 <= cc <= KC + 1:
                corrT_sec(cc - 2)
                corradd_sec(cc - 2)
                qret_sec(cc - 2)
            if 4 <= cc <= KC + 3:
                qst_sec(cc - 4)
            if 2 <= cc <= KC + 1:
                qsq_sec(cc - 2)

            if cc < KC:
                qmm_sec(cc)

            if cc == 1:
                def _(t):
                    wait(t, "L8s")
                    for hf in range(2):
                        t.matmul(P2[0:1, hf * 512:(hf + 1) * 512], ones8[:],
                                 L8[:, hf * 512:(hf + 1) * 512], start=True, stop=True)
                    li = None
                    for hf in range(2):
                        li = t.matmul(P3[0:1, hf * 512:(hf + 1) * 512], ones8[:],
                                      L8[:, 1024 + hf * 512: 1024 + (hf + 1) * 512],
                                      start=True, stop=True)
                    li.then_inc(sT.sem, 1)
                    sT.inc()
                    mark(sT, "saccT")
                TS(_)

        chk("corr", ["corr_done"],
            [(dbg["B"][c * 128:(c + 1) * 128, :], ret(c)) for c in range(KC)]
            if stop == "corr" else [])

        rows_and_bcast("ph5", P0, P1, var_src=P3[32:33, 0:1024])

        for c in range(KC):
            def _(v, c=c):
                if c == 0:
                    wait(v, "ph5_bcast")
                t1 = [U0, U1][c % 2]
                v.drain()
                v.tensor_sub(t1, qret(c), P1[:, 0:1024])
                v.drain()
                v.scalar_tensor_tensor(t1, t1, gcol_sb[:, c:c + 1], P0[:, 0:1024],
                                       ALU.mult, ALU.mult)
                v.drain()
                v.scalar_tensor_tensor(out2(c), t1, bcol_sb[:, c:c + 1], sg(c),
                                       ALU.add, ALU.mult).then_inc(sV.sem, 1)
                sV.inc()
                mark(sV, f"out2_{c}")
                if c == KC - 1:
                    mark(sV, "out2_done")
            VS(_)
        chk("attn", ["out2_done"],
            ([(dbg["A"][c * 128:(c + 1) * 128, :], qret(c)) for c in range(KC)]
             + [(dbg["A"][2048 + c * 128:2048 + (c + 1) * 128, :], sg(c)) for c in range(KC)]
             + [(dbg["A"][4096 + c * 128:4096 + (c + 1) * 128, :], out2(c)) for c in range(KC)]
             ) if stop == "attn" else [])

        def x1st_sec(c):
            def _(t, c=c):
                wait(t, f"x1sq{c}")
                stat_mms(t, x1bf(c), [SQ1, H2T1][c % 2],
                         c == 0, c == KC - 1).then_inc(sT.sem, 1)
                sT.inc()
                mark(sT, f"x1st{c}")
                if c == KC - 1:
                    mark(sT, "ph7_stats")
            TS(_)

        for c in range(KC):
            if c % 2 == 0:
                def _(g, c=c):
                    attn_wblock(g, d_wo, 3, c // 2)
                GS(_)

            def _(y, c=c):
                sl = [sU0c, sU1c][c % 2]
                if c < 2:
                    wait(y, "out2_done")
                else:
                    wait(y, f"x1ev{c-2}")
                y.dma_start(out=[U0, U1][c % 2],
                            in_=d_x[c * 128:(c + 1) * 128, :]).then_inc(sl.sem, 16)
                sl.inc(16)
                mark(sl, f"xre{c}")
            YS(_)

            if c >= 2:
                x1st_sec(c - 2)

            def _(t, c=c):
                slot = (c // 2) % 2
                if c == 0:
                    wait(t, "out2_done")
                wait(t, f"aw3_f{c//2}")
                if c >= 2:
                    wait(t, f"x1ev{c-2}")
                wtile = awslot(slot).rearrange("p (ci j) -> p ci j", ci=16)
                mm_chunk(t, PP[c % 2],
                         lambda ci: wtile[:, ci, (c % 2) * 128:(c % 2) * 128 + 128],
                         lambda ci: out2(ci)).then_inc(sT.sem, 1)
                sT.inc()
                mark(sT, f"womm{c}")
                if c % 2 == 1:
                    mark(sT, f"aw3_b{c//2}")
            TS(_)

            def _(v, c=c):
                wait(v, f"womm{c}")
                wait(v, f"xre{c}")
                if c >= 1:
                    wait(v, f"x1spill{c-1}")
                v.scalar_tensor_tensor(U2, PP[c % 2][:], gaf_c(c), [U0, U1][c % 2],
                                       ALU.mult, ALU.add).then_inc(sV.sem, 1)
                sV.inc()
                mark(sV, f"x1ev{c}")
            VS(_)

            def _(s, c=c):
                wait(s, f"x1ev{c}")
                s.activation(x1bf(c), U2, AFT.Identity).then_inc(sS.sem, 1)
                sS.inc()
                mark(sS, f"x1bf{c}")
                if c >= 2:
                    wait(s, f"x1st{c-2}")
                sq = [SQ1, H2T1][c % 2]
                s.activation(sq, U2, AFT.Square).then_inc(sS.sem, 1)
                sS.inc()
                mark(sS, f"x1sq{c}")
            SS(_)

            def _(y, c=c):
                wait(y, f"x1sq{c}")
                y.dma_start(out=d_x1[c * 128:(c + 1) * 128, :], in_=U2
                            ).then_inc(sSpill.sem, 16)
                sSpill.inc(16)
                mark(sSpill, f"x1spill{c}")
                if c == KC - 1:
                    mark(sSpill, "x1spilled")
            YS(_)

        x1st_sec(KC - 2)
        x1st_sec(KC - 1)

        rows_and_bcast("ph7", P2, P3)

        zsched = _fwht_schedule(KC)
        zslot = {c: c for c in range(KC)}
        zpool = [16, 17]

        def z_ap(idx):
            if idx < 16:
                return z_raw(idx)
            return [SPARE1, SPARE2][idx - 16]

        for c in range(KC):
            def _(v, c=c):
                if c == 0:
                    wait(v, "ph7_bcast")
                    wait(v, "x1spilled")
                t1 = [U0, U1][c % 2]
                if c >= 2:
                    wait(v, f"h2_{c-2}")
                v.drain()
                v.tensor_sub(t1, x1bf(c), P3[:, 0:1024])
                v.drain()
                v.tensor_mul(t1, t1, P2[:, 0:1024]).then_inc(sV.sem, 1)
                sV.inc()
                mark(sV, f"xn2_{c}")
            VS(_)

            if c % 2 == 0:
                continue

            def _(s, c=c):
                wait(s, f"xn2_{c}")
                for c2 in (c - 1, c):
                    if c2 >= 2:
                        wait(s, f"h2mm{c2-2}")
                    s.activation([H2T0, H2T1][c2 % 2], [U0, U1][c2 % 2],
                                 AFT.Identity, scale=scs_c(c2), bias=sbs_c(c2)
                                 ).then_inc(sS.sem, 1)
                    sS.inc()
                    mark(sS, f"h2_{c2}")
            SS(_)

            for c2 in (c - 1, c):
                def _(t, c=c2):
                    wait(t, f"h2_{c}")
                    if c >= 2:
                        wait(t, f"zev{c-2}")
                    h2t = [H2T0, H2T1][c % 2]
                    dst = PP[c % 2]
                    li = None
                    for hf in range(2):
                        li = t.matmul(dst[:, hf * 512:(hf + 1) * 512], h128_sb[:],
                                      h2t[:, hf * 512:(hf + 1) * 512], start=True, stop=True)
                    li.then_inc(sT.sem, 1)
                    sT.inc()
                    mark(sT, f"h2mm{c}")
                TS(_)

                def _(s, c=c2):
                    wait(s, f"h2mm{c}")
                    s.copy(z_raw(c), PP[c % 2][:]).then_inc(sS.sem, 1)
                    sS.inc()
                    mark(sS, f"zev{c}")
                SS(_)

        def _(v):
            wait(v, f"h2_{KC-1}")
            dw, dr = set(), set()
            for ev in zsched:
                if ev[0] == "ready":
                    wait(v, f"zev{ev[1]}")
                else:
                    _t, st_i, a, b = ev
                    ia, ib = zslot[a], zslot[b]
                    oa = zpool.pop(0)
                    ob = zpool.pop(0)
                    if ({ia, ib} & dw) or ({oa, ob} & (dw | dr)):
                        v.drain()
                        dw.clear()
                        dr.clear()
                    v.tensor_add(z_ap(oa), z_ap(ia), z_ap(ib))
                    v.tensor_sub(z_ap(ob), z_ap(ia), z_ap(ib))
                    dw.update((oa, ob))
                    dr.update((ia, ib))
                    zpool.extend([ia, ib])
                    zslot[a], zslot[b] = oa, ob
            for c in range(KC):
                if zslot[c] >= 16:
                    dst = next(i for i in zpool if i < 16)
                    zpool.remove(dst)
                    v.drain()
                    dw.clear()
                    dr.clear()
                    v.tensor_copy(z_ap(dst), z_ap(zslot[c]))
                    dw.add(dst)
                    zpool.append(zslot[c])
                    zslot[c] = dst
            v.drain().then_inc(sV.sem, 1)
            sV.inc()
            mark(sV, "z_done")
        VS(_)
        chk("z", ["z_done"], [(dbg["B"][c * 128:(c + 1) * 128, :], z_ap(zslot[c]))
                              for c in range(KC)] if stop == "z" else [])

        usched = _fwht_schedule(JC)
        uslot = {c: c for c in range(JC)}
        upool = [66, 67, 64, 65]
        u_dw, u_dr = set(), set()
        u_forder = []

        rspare = rowmm.bitcast(BF16)
        def u_ap(idx):
            if idx < 64:
                return u_raw(idx)
            if idx < 66:
                return [SPARE1, SPARE2][idx - 64]
            return rspare[:, (idx - 66) * 1024:(idx - 65) * 1024]

        CAP = 12
        uev_groups = []
        fifo = []
        cur = None
        for ev in usched:
            if ev[0] == "ready":
                if cur is not None:
                    uev_groups.append(cur)
                cur = [ev[1], []]
            else:
                fifo.append(ev)
                continue
            while fifo and len(cur[1]) < CAP:
                cur[1].append(fifo.pop(0))
        uev_groups.append(cur)
        u_tail_ops = fifo

        for jt, ops in uev_groups:
            def _(g, jt=jt):
                for (widx, dsrc, sl) in [(0, d_wup, sWU), (1, d_wgt, sWG)]:
                    slot = jt % 2
                    if jt >= 2:
                        wait(g, f"ug_b{jt-2}")
                    else:
                        wait(g, "aw3_b7")
                    g.dma_start(out=fwslot(widx * 2 + slot),
                                in_=dsrc[jt * 128:(jt + 1) * 128, :]
                                ).then_inc(sl[slot].sem, 16)
                    sl[slot].inc(16)
                    mark(sl[slot], ["wu", "wg"][widx] + str(jt))
            GS(_)

            def _(t, jt=jt):
                slot = jt % 2
                if jt == 0:
                    wait(t, "z_done")
                wait(t, f"wu{jt}")
                wait(t, f"wg{jt}")
                if jt >= 2:
                    wait(t, f"uev{jt-2}")
                pu = PP[slot * 2]
                pg = PP[slot * 2 + 1]
                wu = fwslot(0 * 2 + slot).rearrange("p (ci j) -> p ci j", ci=16)
                wg = fwslot(1 * 2 + slot).rearrange("p (ci j) -> p ci j", ci=16)
                mm_chunk(t, pg, lambda ci: wg[:, ci, :], lambda ci: z_ap(zslot[ci]))
                mm_chunk(t, pu, lambda ci: wu[:, ci, :],
                         lambda ci: z_ap(zslot[ci])).then_inc(sT.sem, 1)
                sT.inc()
                mark(sT, f"ugmm{jt}")
                mark(sT, f"ug_b{jt}")
            TS(_)

            def _(s, jt=jt):
                wait(s, f"ugmm{jt}")
                pg = PP[(jt % 2) * 2 + 1]
                s.activation([U2, U3][jt % 2], pg[:], AFT.Silu,
                             scale=gsc_sb[:, jt:jt + 1]).then_inc(sS.sem, 1)
                sS.inc()
                mark(sS, f"sgate{jt}")
            SS(_)

            def _(v, jt=jt, ops=ops):
                wait(v, f"sgate{jt}")
                pu = PP[(jt % 2) * 2]
                v.scalar_tensor_tensor(u_raw(jt), pu[:], usc_sb[:, jt:jt + 1],
                                       [U2, U3][jt % 2], ALU.mult, ALU.mult
                                       ).then_inc(sV.sem, 1)
                sV.inc()
                mark(sV, f"uev{jt}")
                u_dw.add(jt)
                for ev in ops:
                    _t, st_i, a, b = ev
                    ia, ib = uslot[a], uslot[b]
                    oa = upool.pop(0)
                    ob = upool.pop(0)
                    if ({ia, ib} & u_dw) or ({oa, ob} & (u_dw | u_dr)):
                        v.drain()
                        u_dw.clear()
                        u_dr.clear()
                    v.tensor_add(u_ap(oa), u_ap(ia), u_ap(ib))
                    v.tensor_sub(u_ap(ob), u_ap(ia), u_ap(ib))
                    u_dw.update((oa, ob))
                    u_dr.update((ia, ib))
                    upool.extend([ia, ib])
                    uslot[a], uslot[b] = oa, ob
                    if st_i == 5:
                        v.drain().then_inc(sV.sem, 1)
                        sV.inc()
                        u_dw.clear()
                        u_dr.clear()
                        for x in (a, b):
                            if uslot[x] not in (64, 65):
                                mark(sV, f"ufin{x}")
                                u_forder.append(x)
            VS(_)

        def _(v):
            for ev in u_tail_ops:
                _t, st_i, a, b = ev
                ia, ib = uslot[a], uslot[b]
                oa = upool.pop(0)
                ob = upool.pop(0)
                if ({ia, ib} & u_dw) or ({oa, ob} & (u_dw | u_dr)):
                    v.drain()
                    u_dw.clear()
                    u_dr.clear()
                v.tensor_add(u_ap(oa), u_ap(ia), u_ap(ib))
                v.tensor_sub(u_ap(ob), u_ap(ia), u_ap(ib))
                u_dw.update((oa, ob))
                u_dr.update((ia, ib))
                upool.extend([ia, ib])
                uslot[a], uslot[b] = oa, ob
                if st_i == 5:
                    v.drain().then_inc(sV.sem, 1)
                    sV.inc()
                    u_dw.clear()
                    u_dr.clear()
                    for x in (a, b):
                        if uslot[x] not in (64, 65):
                            mark(sV, f"ufin{x}")
                            u_forder.append(x)
            for c in range(JC):
                if uslot[c] in (64, 65):
                    dst = next(i for i in upool if i < 64 or i >= 66)
                    upool.remove(dst)
                    v.drain()
                    v.tensor_copy(u_ap(dst), u_ap(uslot[c]))
                    upool.append(uslot[c])
                    uslot[c] = dst
                    v.drain().then_inc(sV.sem, 1)
                    sV.inc()
                    mark(sV, f"ufin{c}")
                    u_forder.append(c)
            v.drain().then_inc(sV.sem, 1)
            sV.inc()
            mark(sV, "u64_done")
            assert sorted(u_forder) == list(range(JC))
        VS(_)
        chk("u", ["u64_done"], [(dbg["A"][c * 128:(c + 1) * 128, :], u_ap(uslot[c]))
                                for c in range(JC)] if stop == "u" else [])

        UH_S = 16
        for ci, c in enumerate(u_forder):
            def _(t, c=c, ci=ci):
                wait(t, f"ufin{c}")
                if ci < 4:
                    wait(t, "uev63")
                else:
                    wait(t, f"uhev{u_forder[ci-4]}")
                src = u_ap(uslot[c])
                dst = PP[ci % 4]
                li = None
                for hf in range(2):
                    li = t.matmul(dst[:, hf * 512:(hf + 1) * 512], h128_sb[:],
                                  src[:, hf * 512:(hf + 1) * 512], start=True, stop=True)
                li.then_inc(sT.sem, 1)
                sT.inc()
                mark(sT, f"uhmm{c}")
            TS(_)

            if ci < UH_S:
                def _(s, c=c, ci=ci):
                    wait(s, f"uhmm{c}")
                    s.copy(u_ap(uslot[c]), PP[ci % 4][:]).then_inc(sS.sem, 1)
                    sS.inc()
                    mark(sS, f"uhev{c}")
                    if ci == UH_S - 1:
                        mark(sS, "uhS_done")
                SS(_)
            else:
                def _(v, c=c, ci=ci):
                    wait(v, f"uhmm{c}")
                    v.tensor_copy(u_ap(uslot[c]), PP[ci % 4][:]).then_inc(sV.sem, 1)
                    sV.inc()
                    mark(sV, f"uhev{c}")
                    if ci == JC - 1:
                        mark(sV, "uh_done")
                VS(_)
        chk("uh", ["uh_done"], [(dbg["A"][c * 128:(c + 1) * 128, :], u_ap(uslot[c]))
                                for c in range(JC)] if stop == "uh" else [])

        for dt_i in range(KC):
            def _(g, dt_i=dt_i):
                slot = dt_i % 2
                if dt_i == 0:
                    wait(g, "u64_done")
                if dt_i >= 2:
                    wait(g, f"dn_b{dt_i-2}")
                g.dma_start(out=dwslot(slot),
                            in_=d_wdn[dt_i * 128:(dt_i + 1) * 128, :]
                            ).then_inc(sWD[slot].sem, 16)
                sWD[slot].inc(16)
                mark(sWD[slot], f"wdn{dt_i}")
            GS(_)

            def _(y, dt_i=dt_i):
                sl = [sU0c, sU1c][dt_i % 2]
                if dt_i == 0:
                    wait(y, "x1spilled")
                    wait(y, "u64_done")
                if dt_i >= 2:
                    wait(y, f"outev{dt_i-2}")
                y.dma_start(out=[U0, U1][dt_i % 2],
                            in_=d_x1[dt_i * 128:(dt_i + 1) * 128, :]).then_inc(sl.sem, 16)
                sl.inc(16)
                mark(sl, f"x1re{dt_i}")
            YS(_)

            def _(t, dt_i=dt_i):
                if dt_i == 0:
                    wait(t, "uh_done")
                    wait(t, "uhS_done")
                wait(t, f"wdn{dt_i}")
                if dt_i >= 2:
                    wait(t, f"outev{dt_i-2}")
                dst = PP[dt_i % 2]
                wtile = dwslot(dt_i % 2).rearrange("p (c j) -> p c j", c=64)
                li = None
                for c in range(JC):
                    src = u_ap(uslot[c])
                    for hf in range(2):
                        li = t.matmul(dst[:, hf * 512:(hf + 1) * 512], wtile[:, c, :],
                                      src[:, hf * 512:(hf + 1) * 512],
                                      start=(c == 0), stop=(c == JC - 1))
                li.then_inc(sT.sem, 1)
                sT.inc()
                mark(sT, f"dnmm{dt_i}")
                mark(sT, f"dn_b{dt_i}")
            TS(_)

            def _(v, dt_i=dt_i):
                wait(v, f"dnmm{dt_i}")
                wait(v, f"x1re{dt_i}")
                if dt_i >= 2:
                    wait(v, f"outst{dt_i-2}")
                v.scalar_tensor_tensor([U2, U3][dt_i % 2], PP[dt_i % 2][:],
                                       gwd_sb[:, dt_i:dt_i + 1], [U0, U1][dt_i % 2],
                                       ALU.mult, ALU.add).then_inc(sV.sem, 1)
                sV.inc()
                mark(sV, f"outev{dt_i}")
            VS(_)

            def _(y, dt_i=dt_i):
                wait(y, f"outev{dt_i}")
                so = sOutP[dt_i % 2]
                y.dma_start(out=d_out[dt_i * 128:(dt_i + 1) * 128, :],
                            in_=[U2, U3][dt_i % 2]).then_inc(so.sem, 16)
                so.inc(16)
                mark(so, f"outst{dt_i}")
            YS(_)


    try:
        _phases()
    except _StopEmit:
        pass

    def _(y):
        y.wait_ge(sems["sOut"].sem, sems["sOut"].v)
        y.wait_ge(sems["sOutB"].sem, sems["sOutB"].v)
    YS(_)

    es.close()
    return nc


_NC_CACHE = {}


def _get_nc(debug=False):
    key = bool(debug)
    if key not in _NC_CACHE:
        _NC_CACHE[key] = build_nc(debug)
    return _NC_CACHE[key]


def _hadamard128():
    h = np.array([[1.0]], dtype=np.float32)
    while h.shape[0] < 128:
        h = np.block([[h, h], [h, -h]])
    return h


def make_in_maps(inputs):
    bf = ml_dtypes.bfloat16
    x = np.asarray(inputs["L0_attn_x"], dtype=np.float32)
    x_fm = np.ascontiguousarray(x.reshape(8192, 2048).T)
    wmod = np.asarray(inputs["L0_wmod"], dtype=np.float32)
    cvec = np.asarray(inputs["L0_c"], dtype=np.float32).reshape(2048)
    cols16 = lambda a: np.ascontiguousarray(np.asarray(a, dtype=np.float32).reshape(16, 128).T)
    cols64 = lambda a: np.ascontiguousarray(np.asarray(a, dtype=np.float32).reshape(64, 128).T)
    wsc = np.asarray(inputs["L0_ffn_Wscale_ug"], dtype=np.float32)
    def tile_w(wt, ncol):
        K, Mfull = wt.shape
        nc_, nb = K // 128, Mfull // ncol
        return np.ascontiguousarray(
            wt.reshape(nc_, 128, nb, ncol).transpose(2, 1, 0, 3).reshape(nb * 128, nc_ * ncol))

    shared = {
        "wqT": tile_w(np.asarray(inputs["L0_attn_Wq"], dtype=np.float32).T, 256),
        "wvT": tile_w(np.asarray(inputs["L0_attn_Wv"], dtype=np.float32).T, 256),
        "wgT": tile_w(np.asarray(inputs["L0_attn_Wg"], dtype=np.float32).T, 256),
        "woT": tile_w(np.asarray(inputs["L0_attn_Wo"], dtype=np.float32).T, 256),
        "wupT": tile_w(np.asarray(inputs["L0_ffn_W_up"], dtype=np.float32).T, 128),
        "wgtT": tile_w(np.asarray(inputs["L0_ffn_W_gate"], dtype=np.float32).T, 128),
        "wdnT": tile_w(np.asarray(inputs["L0_ffn_W_down"], dtype=np.float32).T, 128),
        "c_cols": cols16(cvec), "beta_row": np.asarray(inputs["L0_attn_beta"], dtype=np.float32).reshape(1, 16),
        "g_cols": cols16(inputs["L0_attn_ln_gamma"]),
        "b_cols": cols16(inputs["L0_attn_ln_beta"]),
        "su_cols": cols16(inputs["L0_ffn_SU_ug"]),
        "wsuu_cols": cols64(wsc[:8192]), "wsug_cols": cols64(wsc[8192:]),
        "sudp_cols": cols64(inputs["L0_ffn_SU_dp"]),
        "wsdp_cols": cols16(inputs["L0_ffn_Wscale_dp"]),
        "h128": _hadamard128().astype(bf), "ident_bf": np.eye(128, dtype=bf),
        "ident_f32": np.eye(128, dtype=np.float32),
    }
    in_maps = []
    for k in range(NCORES):
        wexp = np.zeros((8, 16), dtype=np.float32)
        wmask = np.zeros((8, 16), dtype=np.float32)
        for j in range(NCORES):
            if j < k:
                wexp[j, :] = 1024.0 * (k - 1 - j)
                wmask[j, :] = 1.0
        m = dict(shared)
        m["x_fm"] = np.ascontiguousarray(x_fm[:, k * T:(k + 1) * T])
        m["wmod_sl"] = tile_w(np.ascontiguousarray(wmod[:, k * MODW:(k + 1) * MODW]), 128)
        m["wexp_row"] = wexp
        m["wmask_row"] = wmask
        in_maps.append(m)
    return in_maps


def kernel(**inputs):
    in_maps = make_in_maps(inputs)
    nc = _get_nc()
    res = run_bass_kernel_spmd(nc, in_maps, core_ids=list(range(NCORES)))
    out_fm = np.concatenate([res.results[k]["OUT"] for k in range(NCORES)], axis=1)
    return np.ascontiguousarray(out_fm.T).reshape(1, 8192, 2048)


# version 18
# speedup vs baseline: 1.0176x; 1.0176x over previous
from contextlib import ExitStack

import numpy as np
import ml_dtypes

import concourse.bass as bass
import concourse.mybir as mybir
from concourse.bass_utils import run_bass_kernel_spmd

F32 = mybir.dt.float32
BF16 = mybir.dt.bfloat16
U8 = mybir.dt.uint8
ALU = mybir.AluOpType
AFT = mybir.ActivationFunctionType

NCORES = 8
D = 2048
T = 1024
KC = 16
JC = 64
MODW = 1536
LN2 = float(np.log(np.float64(2.0)))
EPS = 1e-5
R2048 = float(1.0 / np.sqrt(np.float64(2048.0)))
R8192 = float(1.0 / np.sqrt(np.float64(8192.0)))
TINY = float(np.finfo(np.float32).tiny)


class Ctr:
    def __init__(self, sem):
        self.sem = sem
        self.v = 0

    def inc(self, n=1):
        self.v += n
        return self.v


def _fwht_schedule(n_chunks):
    stages = []
    s = 1
    while s < n_chunks:
        stages.append(s)
        s *= 2
    level = [0] * n_chunks
    pending = []
    for st_i, s in enumerate(stages):
        for a in range(n_chunks):
            if not (a & s):
                pending.append((st_i, a, a | s))
    events = []
    ready = [False] * n_chunks
    for c in range(n_chunks):
        ready[c] = True
        events.append(("ready", c))
        progress = True
        while progress:
            progress = False
            for op in list(pending):
                st_i, a, b = op
                if ready[a] and ready[b] and level[a] == st_i and level[b] == st_i:
                    events.append(("op", st_i, a, b))
                    level[a] = st_i + 1
                    level[b] = st_i + 1
                    pending.remove(op)
                    progress = True
    assert not pending
    return events


class _StopEmit(Exception):
    pass


def build_nc(debug=False, stop=None):
    nc = bass.Bass("TRN2", target_bir_lowering=False)

    d_x = nc.dram_tensor("x_fm", [D, T], F32, kind="ExternalInput").ap()
    d_wq = nc.dram_tensor("wqT", [8 * 128, 16 * 256], F32, kind="ExternalInput").ap()
    d_wv = nc.dram_tensor("wvT", [8 * 128, 16 * 256], F32, kind="ExternalInput").ap()
    d_wg = nc.dram_tensor("wgT", [8 * 128, 16 * 256], F32, kind="ExternalInput").ap()
    d_wo = nc.dram_tensor("woT", [8 * 128, 16 * 256], F32, kind="ExternalInput").ap()
    d_wup = nc.dram_tensor("wupT", [64 * 128, 16 * 128], F32, kind="ExternalInput").ap()
    d_wgt = nc.dram_tensor("wgtT", [64 * 128, 16 * 128], F32, kind="ExternalInput").ap()
    d_wdn = nc.dram_tensor("wdnT", [16 * 128, 64 * 128], F32, kind="ExternalInput").ap()
    d_wmod = nc.dram_tensor("wmod_sl", [12 * 128, 16 * 128], F32, kind="ExternalInput").ap()
    d_ccol = nc.dram_tensor("c_cols", [128, 16], F32, kind="ExternalInput").ap()
    d_beta = nc.dram_tensor("beta_row", [1, 16], F32, kind="ExternalInput").ap()
    d_gcol = nc.dram_tensor("g_cols", [128, 16], F32, kind="ExternalInput").ap()
    d_bcol = nc.dram_tensor("b_cols", [128, 16], F32, kind="ExternalInput").ap()
    d_sucol = nc.dram_tensor("su_cols", [128, 16], F32, kind="ExternalInput").ap()
    d_wsuu = nc.dram_tensor("wsuu_cols", [128, 64], F32, kind="ExternalInput").ap()
    d_wsug = nc.dram_tensor("wsug_cols", [128, 64], F32, kind="ExternalInput").ap()
    d_sudp = nc.dram_tensor("sudp_cols", [128, 64], F32, kind="ExternalInput").ap()
    d_wsdp = nc.dram_tensor("wsdp_cols", [128, 16], F32, kind="ExternalInput").ap()
    d_h128 = nc.dram_tensor("h128", [128, 128], BF16, kind="ExternalInput").ap()
    d_idbf = nc.dram_tensor("ident_bf", [128, 128], BF16, kind="ExternalInput").ap()
    d_idf32 = nc.dram_tensor("ident_f32", [128, 128], F32, kind="ExternalInput").ap()
    d_wexp = nc.dram_tensor("wexp_row", [8, 16], F32, kind="ExternalInput").ap()
    d_wmsk = nc.dram_tensor("wmask_row", [8, 16], F32, kind="ExternalInput").ap()
    d_out = nc.dram_tensor("OUT", [D, T], F32, kind="ExternalOutput").ap()

    d_x1 = nc.dram_tensor("x1_scratch", [D, T], F32).ap()
    d_cc1i = nc.dram_tensor("cc1_in", [12, 128], F32)
    d_cc1o = nc.dram_tensor("cc1_out", [96, 128], F32)
    d_cc2i = nc.dram_tensor("cc2_in", [16, 128], F32)
    d_cc2o = nc.dram_tensor("cc2_out", [8 * 16, 128], F32)

    dbg = {}
    if debug or stop:
        dbg["A"] = nc.dram_tensor("DBG_A", [8192, T], BF16, kind="ExternalOutput").ap()
        dbg["B"] = nc.dram_tensor("DBG_B", [D, T], BF16, kind="ExternalOutput").ap()
        dbg["C"] = nc.dram_tensor("DBG_C", [D, T], BF16, kind="ExternalOutput").ap()
        dbg["S"] = nc.dram_tensor("DBG_S", [128, 256], F32, kind="ExternalOutput").ap()
        dbg["R"] = nc.dram_tensor("DBG_R", [1, 2048], F32, kind="ExternalOutput").ap()

    arena = nc.alloc_sbuf_tensor("arena", [128, 160 * 1024], U8)
    abf = arena.bitcast(BF16)
    af32 = arena.bitcast(F32)
    RA, RB, RC, RD, RE = 0, 16384, 32768, 49152, 65536
    xbf = lambda c: abf[:, RA + c * T: RA + (c + 1) * T]
    qret = lambda c: abf[:, RA + c * T: RA + (c + 1) * T]
    x1bf = lambda c: abf[:, RA + c * T: RA + (c + 1) * T]
    out2 = lambda c: abf[:, RB + c * T: RB + (c + 1) * T]
    wmod_sb = af32[:, 16384: 16384 + KC * MODW]
    ret = lambda c: abf[:, RC + c * T: RC + (c + 1) * T]
    sg = lambda c: abf[:, RD + c * T: RD + (c + 1) * T]
    h_sl = lambda c: abf[:, RE + c * T: RE + (c + 1) * T]
    z_raw = lambda i: abf[:, RE + i * T: RE + (i + 1) * T]
    u_raw = lambda i: abf[:, RA + i * T: RA + (i + 1) * T]
    dwslot = lambda s: abf[:, RE + s * 8192: RE + (s + 1) * 8192]

    un = nc.alloc_sbuf_tensor("units", [128, 4 * 1024], F32)
    U0, U1 = un[:, 0:1024], un[:, 1024:2048]
    U2, U3 = un[:, 2048:3072], un[:, 3072:4096]
    un_bf = un.bitcast(BF16)
    SPARE1 = un_bf[:, 0:1024]
    SPARE2 = un_bf[:, 1024:2048]
    SQ0 = un_bf[:, 4096:5120]
    SQ1 = un_bf[:, 6144:7168]
    H2T0 = un_bf[:, 5120:6144]
    H2T1 = un_bf[:, 7168:8192]

    rowmm = nc.alloc_sbuf_tensor("rowmm", [128, 2048], F32)
    row_rsq = rowmm[0:1, 0:1024]
    row_mu = rowmm[0:1, 1024:2048]
    sacc = rowmm[0:1, 0:2048]
    w8_sb = nc.alloc_sbuf_tensor("w8_sb", [8, 16], F32)
    lnl8_sb = nc.alloc_sbuf_tensor("lnl8_sb", [8, 16], F32)
    ones8 = nc.alloc_sbuf_tensor("ones8", [8, 1], F32)
    piece0 = un[0:1, 0:2048]
    piece1 = un[0:1, 2048:4096]
    prow0 = un[0:1, 2048:3072]
    prow1 = un[0:1, 3072:4096]
    row_var = un[0:1, 3072:4096]

    wbuf = nc.alloc_sbuf_tensor("wbuf", [128, 16 * 1024], U8)
    wb_bf = wbuf.bitcast(BF16)
    awslot = lambda s: wb_bf[:, s * 4096:(s + 1) * 4096]
    fwslot = lambda s: wb_bf[:, s * 2048:(s + 1) * 2048]

    h128_sb = nc.alloc_sbuf_tensor("h128_sb", [128, 128], BF16)
    idbf_sb = nc.alloc_sbuf_tensor("idbf_sb", [128, 128], BF16)
    idf32_sb = nc.alloc_sbuf_tensor("idf32_sb", [128, 128], F32)
    ccol_sb = nc.alloc_sbuf_tensor("ccol_sb", [128, 16], F32)
    cbf_sb = ccol_sb
    beta_sb = nc.alloc_sbuf_tensor("beta_sb", [1, 16], F32)
    lam_row = nc.alloc_sbuf_tensor("lam_row", [1, 16], F32)
    lnlam_row = nc.alloc_sbuf_tensor("lnlam_row", [1, 16], F32)
    lam_cols = nc.alloc_sbuf_tensor("lam_cols", [128, 16], F32)
    gcol_sb = nc.alloc_sbuf_tensor("gcol_sb", [128, 16], F32)
    bcol_sb = nc.alloc_sbuf_tensor("bcol_sb", [128, 16], F32)
    sucol_sb = nc.alloc_sbuf_tensor("sucol_sb", [128, 16], F32)
    wsuu_sb = nc.alloc_sbuf_tensor("wsuu_sb", [128, 64], F32)
    wsug_sb = nc.alloc_sbuf_tensor("wsug_sb", [128, 64], F32)
    sudp_sb = nc.alloc_sbuf_tensor("sudp_sb", [128, 64], F32)
    wsdp_sb = nc.alloc_sbuf_tensor("wsdp_sb", [128, 16], F32)
    usc_sb = nc.alloc_sbuf_tensor("usc_sb", [128, 64], F32)
    gsc_sb = nc.alloc_sbuf_tensor("gsc_sb", [128, 64], F32)
    gwd_sb = nc.alloc_sbuf_tensor("gwd_sb", [128, 16], F32)
    modc_sb = nc.alloc_sbuf_tensor("modc_sb", [128, 96], F32)
    modp_sb = nc.alloc_sbuf_tensor("modp_sb", [128, 96], F32)
    scr16 = un[:, 2048:2064]
    scr16b = un[:, 3072:3088]
    ones_col = nc.alloc_sbuf_tensor("ones_col", [128, 1], BF16)
    ones_row = nc.alloc_sbuf_tensor("ones_row", [1, 128], F32)
    wexp_sb = rowmm[0:8, 2304:2320].transpose([0, 1]) if False else None
    wexp_sb = nc.alloc_sbuf_tensor("wexp_sb", [8, 16], F32)
    wmsk_sb = nc.alloc_sbuf_tensor("wmsk_sb", [8, 16], F32)
    modsl_sb = nc.alloc_sbuf_tensor("modsl_sb", [128, 12], F32)
    modtr_sb = nc.alloc_sbuf_tensor("modtr_sb", [12, 128], F32)
    cc1_sb = nc.alloc_sbuf_tensor("cc1_sb", [96, 128], F32)
    lbuf_sb = nc.alloc_sbuf_tensor("lbuf_sb", [128, 16], BF16)
    ltr_sb = nc.alloc_sbuf_tensor("ltr_sb", [16, 128], F32)

    sa_c = lambda c: modp_sb[:, 0 + c: 1 + c]
    ca1_c = lambda c: modp_sb[:, 16 + c: 17 + c]
    gaf_c = lambda c: modp_sb[:, 32 + c: 33 + c]
    scs_c = lambda c: modp_sb[:, 48 + c: 49 + c]
    sbs_c = lambda c: modp_sb[:, 64 + c: 65 + c]

    es = ExitStack()
    P0 = es.enter_context(nc.psum_tensor("P0", [128, 1024], F32))
    P1 = es.enter_context(nc.psum_tensor("P1", [128, 1024], F32))
    P2 = es.enter_context(nc.psum_tensor("P2", [128, 1024], F32))
    P3 = es.enter_context(nc.psum_tensor("P3", [128, 1024], F32))
    PP = [P0, P1, P2, P3]
    row_t1 = P1[0:1, 0:1024]

    sems = {}
    for i in range(12):
        sems[f"sWM{i}"] = Ctr(es.enter_context(nc.semaphore(f"sWM{i}")))
    sWMs = [sems[f"sWM{i}"] for i in range(12)]
    for nm in ["sMisc", "sU0", "sU1", "sSpill", "sSpillB", "sOut", "sOutB", "sWM", "sCC", "sPC", "sPCB", "sGP",
               "sWA0", "sWA1", "sWU0", "sWU1", "sWG0", "sWG1", "sWD0", "sWD1",
               "sT", "sV", "sS", "sX2"]:
        sems[nm] = Ctr(es.enter_context(nc.semaphore(nm)))
    sMisc, sU0c, sU1c = sems["sMisc"], sems["sU0"], sems["sU1"]
    sSpill, sOut, sWM, sCC = sems["sSpill"], sems["sOut"], sems["sWM"], sems["sCC"]
    sPC = [sems["sPC"], sems["sPCB"]]
    sOutP = [sems["sOut"], sems["sOutB"]]
    sWA = [sems["sWA0"], sems["sWA1"]]
    sWU = [sems["sWU0"], sems["sWU1"]]
    sWG = [sems["sWG0"], sems["sWG1"]]
    sWD = [sems["sWD0"], sems["sWD1"]]
    sT, sV, sS, sX2 = sems["sT"], sems["sV"], sems["sS"], sems["sX2"]

    block = es.enter_context(nc.Block())

    M = {}

    def mark(ctr, name):
        M[name] = (ctr, ctr.v)

    def wait(eng, name):
        ctr, v = M[name]
        if v > 0:
            eng.wait_ge(ctr.sem, v)

    TS, VS, SS, GS, YS = block.tensor, block.vector, block.scalar, block.gpsimd, block.sync

    sDbg = Ctr(es.enter_context(nc.semaphore("sDbg")))

    def chk(name, waits, dumps):
        if stop != name:
            return

        def _(y):
            for w in waits:
                wait(y, w)
            for (dst, src) in dumps:
                y.dma_start(out=dst, in_=src).then_inc(sDbg.sem, 16)
                sDbg.inc(16)
            y.wait_ge(sDbg.sem, sDbg.v)
        YS(_)
        raise _StopEmit()

    def _phases():
        def attn_wblock(g, dsrc, widx, b):
            slot = b % 2
            if b >= 2:
                wait(g, f"aw{widx}_b{b-2}")
            elif widx > 0:
                wait(g, f"aw{widx-1}_b{6 + b}")
            g.dma_start(out=awslot(slot),
                        in_=dsrc[b * 128:(b + 1) * 128, :]).then_inc(sWA[slot].sem, 16)
            sWA[slot].inc(16)
            mark(sWA[slot], f"aw{widx}_f{b}")

        def mm_chunk(t, dst, wtile_fn, rhs_fn):
            last = None
            for ci in range(KC):
                lhs = wtile_fn(ci)
                rhs = rhs_fn(ci)
                for hf in range(2):
                    last = t.matmul(dst[:, hf * 512:(hf + 1) * 512], lhs,
                                    rhs[:, hf * 512:(hf + 1) * 512],
                                    start=(ci == 0), stop=(ci == KC - 1))
            return last

        def stat_mms(t, src_bf, sq_bf, first, last, sq_dst=None):
            sq_dst = P2[0:1, 0:1024] if sq_dst is None else sq_dst
            li = None
            for hf in range(2):
                t.matmul(P3[0:1, hf * 512:(hf + 1) * 512], ones_col[:],
                         src_bf[:, hf * 512:(hf + 1) * 512], start=first, stop=last)
            for hf in range(2):
                li = t.matmul(sq_dst[0:1, hf * 512:(hf + 1) * 512], ones_col[:],
                              sq_bf[:, hf * 512:(hf + 1) * 512], start=first, stop=last)
            return li

        def rows_and_bcast(phase, pb_rsq, pb_mu, var_src=None, also_wait=None):
            def _(v):
                wait(v, f"{phase}_stats")
                for aw in (also_wait or []):
                    wait(v, aw)
                v.tensor_scalar_mul(row_mu[:], P3[0:1, 0:1024], 1.0 / D)
                v.tensor_scalar_mul(row_var[:], (P2[0:1, 0:1024] if var_src is None
                                                 else var_src), 1.0 / D)
                v.drain()
                v.tensor_mul(row_t1[:], row_mu[:], row_mu[:])
                v.drain()
                v.tensor_sub(row_var[:], row_var[:], row_t1[:])
                v.drain()
                v.tensor_scalar(row_var[:], row_var[:], EPS, None, ALU.add)
                v.drain().then_inc(sV.sem, 1)
                sV.inc()
                mark(sV, f"{phase}_var")
            VS(_)

            def _(s):
                wait(s, f"{phase}_var")
                s.activation(row_t1[:], row_var[:], AFT.Ln)
                s.drain()
                s.activation(row_rsq[:], row_t1[:], AFT.Exp, scale=-0.5).then_inc(sS.sem, 1)
                sS.inc()
                mark(sS, f"{phase}_rows")
            SS(_)

            def _(t):
                wait(t, f"{phase}_rows")
                li = None
                for hf in range(2):
                    t.matmul(pb_rsq[:, hf * 512:(hf + 1) * 512], ones_row[:],
                             row_rsq[:, hf * 512:(hf + 1) * 512], start=True, stop=True)
                    li = t.matmul(pb_mu[:, hf * 512:(hf + 1) * 512], ones_row[:],
                                  row_mu[:, hf * 512:(hf + 1) * 512], start=True, stop=True)
                li.then_inc(sT.sem, 1)
                sT.inc()
                mark(sT, f"{phase}_bcast")
            TS(_)




        def _(y):
            for (dst, src) in [
                (h128_sb[:], d_h128), (idbf_sb[:], d_idbf), (idf32_sb[:], d_idf32),
                (ccol_sb[:], d_ccol),
                (beta_sb[:], d_beta), (gcol_sb[:], d_gcol), (bcol_sb[:], d_bcol),
                (sucol_sb[:], d_sucol), (wsuu_sb[:], d_wsuu), (wsug_sb[:], d_wsug),
                (sudp_sb[:], d_sudp), (wsdp_sb[:], d_wsdp),
                (wexp_sb[:], d_wexp), (wmsk_sb[:], d_wmsk),
            ]:
                y.dma_start(out=dst, in_=src).then_inc(sMisc.sem, 16)
                sMisc.inc(16)
            mark(sMisc, "consts")
        YS(_)

        def _(x):
            for m in range(12):
                x.dma_start(out=wmod_sb[:, m * 2048:(m + 1) * 2048],
                            in_=d_wmod[m * 128:(m + 1) * 128, :]).then_inc(sWMs[m].sem, 16)
                sWMs[m].inc(16)
                mark(sWMs[m], f"wmod{m}")
        SS(_)

        def _(v):
            wait(v, "consts")
            v.memset(ones_col[:], 1.0)
            v.memset(ones_row[:], 1.0)
            v.memset(ones8[:], 1.0)
            v.tensor_scalar_mul(lam_row[:], beta_sb[:], -LN2).then_inc(sV.sem, 1)
            sV.inc()
            mark(sV, "lam_pre")
        VS(_)

        def _(s):
            wait(s, "lam_pre")
            s.activation(lam_row[:], lam_row[:], AFT.Exp).then_inc(sS.sem, 1)
            sS.inc()
            mark(sS, "exp_lam")
            s.activation(cbf_sb[:], ccol_sb[:], AFT.Silu).then_inc(sS.sem, 1)
            sS.inc()
            mark(sS, "siluc")
        SS(_)

        def _(v):
            wait(v, "exp_lam")
            v.tensor_scalar(lam_row[:], lam_row[:], -1.0, 1.0, ALU.mult, ALU.add)
            v.drain()
            v.tensor_scalar(lam_row[:], lam_row[:], TINY, 1.0, ALU.max, ALU.min)
            v.drain()
            v.tensor_copy(lnlam_row[:], lam_row[:])
            v.drain().then_inc(sV.sem, 1)
            sV.inc()
            mark(sV, "lam_done")
        VS(_)

        def _(s):
            wait(s, "lam_done")
            s.activation(lnlam_row[:], lnlam_row[:], AFT.Ln).then_inc(sS.sem, 1)
            sS.inc()
            mark(sS, "lnlam")
        SS(_)

        def _(t):
            wait(t, "lam_done")
            t.matmul(P2[0:128, 0:16], ones_row[:], lam_row[:], start=True,
                     stop=True)
            wait(t, "lnlam")
            t.matmul(P2[0:8, 16:32], ones_row[:, 0:8], lnlam_row[:], start=True,
                     stop=True).then_inc(sT.sem, 1)
            sT.inc()
            mark(sT, "lamb")
        TS(_)

        def _(s):
            wait(s, "lamb")
            s.copy(lam_cols[:], P2[0:128, 0:16])
            s.copy(lnl8_sb[:], P2[0:8, 16:32]).then_inc(sS.sem, 1)
            sS.inc()
            mark(sS, "lamcols")
        SS(_)

        MODPP = [P0, P1, P3]
        for m in range(12):
            def _(t, m=m):
                if m == 0:
                    wait(t, "siluc")
                wait(t, f"wmod{m}")
                if m >= 3:
                    wait(t, f"modev{m-3}")
                pcol = MODPP[m % 3][:, 0:1]
                li = None
                for ci in range(KC):
                    li = t.matmul(pcol,
                                  wmod_sb[:, m * 2048 + ci * 128: m * 2048 + (ci + 1) * 128],
                                  cbf_sb[:, ci:ci + 1],
                                  start=(ci == 0), stop=(ci == KC - 1))
                li.then_inc(sT.sem, 1)
                sT.inc()
                mark(sT, f"modmm{m}")
            TS(_)

            def _(s, m=m):
                wait(s, f"modmm{m}")
                s.copy(modsl_sb[:, m:m + 1], MODPP[m % 3][:, 0:1]).then_inc(sS.sem, 1)
                sS.inc()
                mark(sS, f"modev{m}")
            SS(_)

        def _(t):
            wait(t, "modev11")
            t.transpose(P2[0:12, 0:128], modsl_sb[:], idf32_sb[:]).then_inc(sT.sem, 1)
            sT.inc()
            mark(sT, "modtrT")
        TS(_)

        def _(s):
            wait(s, "modtrT")
            s.copy(modtr_sb[:], P2[0:12, 0:128]).then_inc(sS.sem, 1)
            sS.inc()
            mark(sS, "modtr")
            s.drain()
            s.dma_start(out=d_cc1i[:, :], in_=modtr_sb[:]).then_inc(sX2.sem, 16)
            sX2.inc(16)
            mark(sX2, "cc1in")
        SS(_)

        def _(g):
            wait(g, "cc1in")
            g.collective_compute("AllGather", ALU.bypass,
                                 replica_groups=[list(range(NCORES))],
                                 ins=[d_cc1i.ap().opt()], outs=[d_cc1o.ap().opt()],
                                 ).then_inc(sCC.sem, 1)
            sCC.inc(1)
            mark(sCC, "cc1")
        GS(_)

        for c in range(KC):
            def _(y, c=c):
                sl = [sU0c, sU1c][c % 2]
                if c >= 2:
                    wait(y, f"xcast{c-2}")
                y.dma_start(out=[U0, U1][c % 2],
                            in_=d_x[c * 128:(c + 1) * 128, :]).then_inc(sl.sem, 16)
                sl.inc(16)
                mark(sl, f"xload{c}")
            YS(_)

            def _(s, c=c):
                wait(s, f"xload{c}")
                s.copy(xbf(c), [U0, U1][c % 2]).then_inc(sS.sem, 1)
                sS.inc()
                mark(sS, f"xcast{c}")
            SS(_)

            def _(v, c=c):
                wait(v, f"xcast{c}")
                if c >= 2:
                    wait(v, f"xst{c-2}")
                sq = [SQ0, SQ1][c % 2]
                v.tensor_mul(sq, xbf(c), xbf(c)).then_inc(sV.sem, 1)
                sV.inc()
                mark(sV, f"xsq{c}")
            VS(_)

            def _(t, c=c):
                wait(t, f"xsq{c}")
                if c == 0:
                    wait(t, "modtr")
                stat_mms(t, xbf(c), [SQ0, SQ1][c % 2],
                         c == 0, c == KC - 1).then_inc(sT.sem, 1)
                sT.inc()
                mark(sT, f"xst{c}")
                if c == KC - 1:
                    mark(sT, "ph1_stats")
            TS(_)

        rows_and_bcast("ph1", P2, P3)


        def _(y):
            wait(y, "cc1")
            y.dma_start(out=cc1_sb[:], in_=d_cc1o.ap()).then_inc(sX2.sem, 16)
            sX2.inc(16)
            mark(sX2, "cc1sb")
        YS(_)

        def _(t):
            wait(t, "cc1sb")
            t.transpose(P0[0:128, 512:608], cc1_sb[:],
                        idf32_sb[0:96, 0:96]).then_inc(sT.sem, 1)
            sT.inc()
            mark(sT, "modcT")
        TS(_)

        def _(s):
            wait(s, "modcT")
            s.copy(modc_sb[:], P0[0:128, 512:608]).then_inc(sS.sem, 1)
            sS.inc()
            mark(sS, "modc")
        SS(_)

        def _(v):
            wait(v, "modc")
            v.tensor_copy(modp_sb[:, 0:16], modc_sb[:, 0:16])
            v.tensor_scalar(modp_sb[:, 16:32], modc_sb[:, 16:32], 1.0, None, ALU.add)
            for (src_g, dst) in [(2, 32), (5, 80)]:
                gac = modc_sb[:, src_g * 16:(src_g + 1) * 16]
                v.drain()
                v.tensor_single_scalar(scr16[:], gac, 0.0, ALU.is_ge)
                v.drain()
                v.tensor_scalar(scr16[:], scr16[:], 2.0, -1.0, ALU.mult, ALU.add)
                v.drain()
                v.tensor_mul(scr16b[:], gac, scr16[:])
                v.drain()
                v.tensor_scalar(scr16b[:], scr16b[:], 0.25, None, ALU.max)
                v.drain()
                v.tensor_mul(modp_sb[:, dst:dst + 16], scr16[:], scr16b[:])
            v.drain()
            v.tensor_scalar(scr16[:], modc_sb[:, 64:80], 1.0, None, ALU.add)
            v.drain()
            v.tensor_mul(modp_sb[:, 48:64], scr16[:], sucol_sb[:])
            v.tensor_mul(modp_sb[:, 64:80], modc_sb[:, 48:64], sucol_sb[:])
            v.tensor_mul(gwd_sb[:], modp_sb[:, 80:96], wsdp_sb[:])
            v.drain()
            v.tensor_scalar_mul(gwd_sb[:], gwd_sb[:], R8192)
            v.tensor_mul(usc_sb[:], wsuu_sb[:], sudp_sb[:])
            v.drain()
            v.tensor_scalar_mul(usc_sb[:], usc_sb[:], R2048)
            v.tensor_scalar_mul(gsc_sb[:], wsug_sb[:], R2048)
            v.drain().then_inc(sV.sem, 1)
            sV.inc()
            mark(sV, "mod_ready")
        VS(_)
        chk("mod", ["mod_ready"], [
            (dbg["S"][:, 0:96], modp_sb[:]),
            (dbg["S"][:, 96:112], lam_cols[:]),
            (dbg["S"][:, 112:128], gwd_sb[:]),
            (dbg["S"][:, 128:192], usc_sb[:]),
            (dbg["S"][:, 192:256], gsc_sb[:]),
        ] if stop == "mod" else [])

        def _(v):
            wait(v, "lamcols")
            wait(v, "consts")
            v.tensor_tensor(w8_sb[:], wexp_sb[:], lnl8_sb[:], ALU.mult
                            ).then_inc(sV.sem, 1)
            sV.inc()
            mark(sV, "w8log")
        VS(_)

        def _(s):
            wait(s, "w8log")
            s.activation(w8_sb[:], w8_sb[:], AFT.Exp).then_inc(sS.sem, 1)
            sS.inc()
            mark(sS, "w8exp")
        SS(_)

        def _(v):
            wait(v, "w8exp")
            v.tensor_mul(w8_sb[:], w8_sb[:], wmsk_sb[:])
            v.drain().then_inc(sV.sem, 1)
            sV.inc()
            mark(sV, "w8")
        VS(_)

        for c in range(KC):
            def _(v, c=c):
                if c == 0:
                    wait(v, "ph1_bcast")
                if c >= 2:
                    wait(v, f"h{c-2}")
                t1 = [U0, U1][c % 2]
                v.tensor_sub(t1, xbf(c), P3[:, 0:1024])
                v.drain()
                v.tensor_mul(t1, t1, P2[:, 0:1024]).then_inc(sV.sem, 1)
                sV.inc()
                mark(sV, f"xn{c}")
            VS(_)

            if c % 2 == 1:
                def _(s, c=c):
                    if c == 1:
                        wait(s, "mod_ready")
                    wait(s, f"xn{c}")
                    for cc2 in (c - 1, c):
                        s.activation(h_sl(cc2), [U0, U1][cc2 % 2], AFT.Identity,
                                     scale=ca1_c(cc2), bias=sa_c(cc2)
                                     ).then_inc(sS.sem, 1)
                        sS.inc()
                        mark(sS, f"h{cc2}")
                    if c == KC - 1:
                        mark(sS, "h_done")
                SS(_)

        chk("h", ["h_done"], [(dbg["B"][c * 128:(c + 1) * 128, :], h_sl(c))
                              for c in range(KC)] if stop == "h" else [])


        for c in range(KC):
            if c % 2 == 0:
                def _(g, c=c):
                    attn_wblock(g, d_wv, 0, c // 2)
                GS(_)

            def _(t, c=c):
                slot = (c // 2) % 2
                if c == 0:
                    wait(t, "h_done")
                wait(t, f"aw0_f{c//2}")
                if c >= 2:
                    wait(t, f"scan{c-2}")
                wtile = awslot(slot).rearrange("p (ci j) -> p ci j", ci=16)
                mm_chunk(t, PP[c % 2],
                         lambda ci: wtile[:, ci, (c % 2) * 128:(c % 2) * 128 + 128],
                         lambda ci: h_sl(ci)).then_inc(sT.sem, 1)
                sT.inc()
                mark(sT, f"vmm{c}")
                if c % 2 == 1:
                    mark(sT, f"aw0_b{c//2}")
            TS(_)

            def _(v, c=c):
                wait(v, f"vmm{c}")
                v.tensor_tensor_scan(ret(c), lam_cols[:, c:c + 1].broadcast_to([128, 1024]),
                                     PP[c % 2][:], 0.0, ALU.mult, ALU.add
                                     ).then_inc(sV.sem, 1)
                sV.inc()
                mark(sV, f"scan{c}")
            VS(_)

        def _(v):
            retv = abf[:, RC: RC + KC * T].rearrange("p (c t) -> p c t", c=KC)
            v.drain()
            v.tensor_copy(lbuf_sb[:], retv[:, :, 1023]).then_inc(sV.sem, 1)
            sV.inc()
            mark(sV, "lbuf")
        VS(_)
        chk("scan", ["lbuf"], [(dbg["B"][c * 128:(c + 1) * 128, :], ret(c))
                               for c in range(KC)] if stop == "scan" else [])

        for c in range(KC):
            if c % 2 == 0:
                def _(g, c=c):
                    attn_wblock(g, d_wg, 1, c // 2)
                GS(_)

            def _(t, c=c):
                slot = (c // 2) % 2
                wait(t, f"aw1_f{c//2}")
                if c >= 2:
                    wait(t, f"sgev{c-2}")
                else:
                    wait(t, f"scan{14 + c}")
                wtile = awslot(slot).rearrange("p (ci j) -> p ci j", ci=16)
                mm_chunk(t, PP[c % 2],
                         lambda ci: wtile[:, ci, (c % 2) * 128:(c % 2) * 128 + 128],
                         lambda ci: h_sl(ci)).then_inc(sT.sem, 1)
                sT.inc()
                mark(sT, f"gmm{c}")
                if c % 2 == 1:
                    mark(sT, f"aw1_b{c//2}")
            TS(_)

            def _(s, c=c):
                wait(s, f"gmm{c}")
                s.activation(sg(c), PP[c % 2][:], AFT.Silu).then_inc(sS.sem, 1)
                sS.inc()
                mark(sS, f"sgev{c}")
            SS(_)

            if c == 1:
                def _(t):
                    wait(t, "lbuf")
                    t.transpose(P2.bitcast(BF16)[0:16, 0:128], lbuf_sb[:],
                                idbf_sb[:]).then_inc(sT.sem, 1)
                    sT.inc()
                    mark(sT, "ltrT")
                TS(_)

                def _(s):
                    wait(s, "ltrT")
                    s.copy(ltr_sb[:], P2.bitcast(BF16)[0:16, 0:128]).then_inc(sS.sem, 1)
                    sS.inc()
                    mark(sS, "ltr")
                SS(_)

                def _(y):
                    wait(y, "ltr")
                    y.dma_start(out=d_cc2i[:, :], in_=ltr_sb[:]).then_inc(sX2.sem, 16)
                    sX2.inc(16)
                    mark(sX2, "cc2in")
                YS(_)

        sacc_bf = un_bf[0:1, 0:2048]
        rowmm_bf = rowmm.bitcast(BF16)
        prow_bf = [rowmm_bf[0:1, 0:1024], rowmm_bf[0:1, 1024:2048]]

        def _(g):
            wait(g, "cc2in")
            g.collective_compute("AllGather", ALU.bypass,
                                 replica_groups=[list(range(NCORES))],
                                 ins=[d_cc2i.ap().opt()], outs=[d_cc2o.ap().opt()],
                                 ).then_inc(sCC.sem, 1)
            sCC.inc(1)
            mark(sCC, "cc2")
            attn_wblock(g, d_wq, 2, 0)
            attn_wblock(g, d_wq, 2, 1)
        GS(_)

        L8 = af32[0:8, 0:2048]

        def _(y):
            wait(y, "cc2")
            y.dma_start(out=L8, in_=d_cc2o.ap()
                        .rearrange("(j c) p -> j (c p)", c=16)).then_inc(sX2.sem, 16)
            sX2.inc(16)
            mark(sX2, "L8")
        YS(_)

        def corrT_sec(c):
            def _(t, c=c):
                if c == 0:
                    wait(t, "sacc")
                wait(t, f"prow{c}")
                if c >= 1:
                    wait(t, f"corradd{c-1}")
                pr = prow_bf[c % 2]
                li = None
                for hf in range(2):
                    li = t.matmul(P2[:, hf * 512:(hf + 1) * 512],
                                  sacc_bf[:, c * 128:(c + 1) * 128],
                                  pr[:, hf * 512:(hf + 1) * 512], start=True, stop=True)
                li.then_inc(sT.sem, 1)
                sT.inc()
                mark(sT, f"corrT{c}")
            TS(_)

        def qst_sec(c):
            def _(t, c=c):
                wait(t, f"qsq{c}")
                stat_mms(t, qret(c), [SQ0, SQ1][c % 2],
                         c == 0, c == KC - 1, sq_dst=P3[32:33]).then_inc(sT.sem, 1)
                sT.inc()
                mark(sT, f"qst{c}")
                if c == KC - 1:
                    mark(sT, "ph5_stats")
            TS(_)

        def prow_sec(c):
            def _(v, c=c):
                if c >= 2:
                    wait(v, f"corrT{c-2}")
                else:
                    v.drain()
                pr = prow_bf[c % 2]
                lamb = lam_row[:, c:c + 1].broadcast_to([1, 1024])
                v.tensor_tensor_scan(pr, lamb, lamb, 1.0, ALU.mult, ALU.bypass
                                     ).then_inc(sV.sem, 1)
                sV.inc()
                mark(sV, f"prow{c}")
            VS(_)

        def corradd_sec(c):
            def _(v, c=c):
                wait(v, f"corrT{c}")
                v.drain()
                v.tensor_add(ret(c), ret(c), P2[:]).then_inc(sV.sem, 1)
                sV.inc()
                mark(sV, f"corradd{c}")
                if c == KC - 1:
                    mark(sV, "corr_done")
            VS(_)

        def qret_sec(c):
            def _(v, c=c):
                wait(v, f"qmm{c}")
                v.drain()
                v.tensor_mul(qret(c), PP[c % 2][:], ret(c)).then_inc(sV.sem, 1)
                sV.inc()
                mark(sV, f"qret{c}")
            VS(_)

        def qsq_sec(c):
            def _(v, c=c):
                if c >= 2:
                    wait(v, f"qst{c-2}")
                sq = [SQ0, SQ1][c % 2]
                v.drain()
                v.tensor_mul(sq, qret(c), qret(c)).then_inc(sV.sem, 1)
                sV.inc()
                mark(sV, f"qsq{c}")
            VS(_)

        def qmm_sec(c):
            def _(t, c=c):
                slot = (c // 2) % 2
                wait(t, f"aw2_f{c//2}")
                if c >= 2:
                    wait(t, f"qret{c-2}")
                else:
                    wait(t, f"sgev{14 + c}")
                wtile = awslot(slot).rearrange("p (ci j) -> p ci j", ci=16)
                mm_chunk(t, PP[c % 2],
                         lambda ci: wtile[:, ci, (c % 2) * 128:(c % 2) * 128 + 128],
                         lambda ci: h_sl(ci)).then_inc(sT.sem, 1)
                sT.inc()
                mark(sT, f"qmm{c}")
                if c % 2 == 1:
                    mark(sT, f"aw2_b{c//2}")
            TS(_)

        for cc in range(KC + 4):
            if cc % 2 == 0 and 4 <= cc < KC:
                def _(g, cc=cc):
                    attn_wblock(g, d_wq, 2, cc // 2)
                GS(_)

            if 1 <= cc <= KC:
                prow_sec(cc - 1)

            if cc == 1:
                def _(v):
                    wait(v, "L8")
                    wait(v, "w8")
                    v.tensor_tensor(
                        L8.rearrange("j (c p) -> j c p", c=16),
                        L8.rearrange("j (c p) -> j c p", c=16),
                        w8_sb[:].unsqueeze(2).broadcast_to([8, 16, 128]), ALU.mult)
                    v.drain().then_inc(sV.sem, 1)
                    sV.inc()
                    mark(sV, "L8s")
                VS(_)

            if cc == 2:
                def _(v):
                    wait(v, "saccT")
                    v.tensor_copy(sacc_bf[0:1, 0:1024], P2[0:1, 0:1024])
                    v.tensor_copy(sacc_bf[0:1, 1024:2048], P3[0:1, 0:1024])
                    v.drain().then_inc(sV.sem, 1)
                    sV.inc()
                    mark(sV, "sacc")
                VS(_)

            if 2 <= cc <= KC + 1:
                corrT_sec(cc - 2)
                corradd_sec(cc - 2)
                qret_sec(cc - 2)
            if 4 <= cc <= KC + 3:
                qst_sec(cc - 4)
            if 2 <= cc <= KC + 1:
                qsq_sec(cc - 2)

            if cc < KC:
                qmm_sec(cc)

            if cc == 1:
                def _(t):
                    wait(t, "L8s")
                    for hf in range(2):
                        t.matmul(P2[0:1, hf * 512:(hf + 1) * 512], ones8[:],
                                 L8[:, hf * 512:(hf + 1) * 512], start=True, stop=True)
                    li = None
                    for hf in range(2):
                        li = t.matmul(P3[0:1, hf * 512:(hf + 1) * 512], ones8[:],
                                      L8[:, 1024 + hf * 512: 1024 + (hf + 1) * 512],
                                      start=True, stop=True)
                    li.then_inc(sT.sem, 1)
                    sT.inc()
                    mark(sT, "saccT")
                TS(_)

        chk("corr", ["corr_done"],
            [(dbg["B"][c * 128:(c + 1) * 128, :], ret(c)) for c in range(KC)]
            if stop == "corr" else [])

        rows_and_bcast("ph5", P0, P1, var_src=P3[32:33, 0:1024])

        for c in range(KC):
            def _(v, c=c):
                if c == 0:
                    wait(v, "ph5_bcast")
                t1 = [U0, U1][c % 2]
                v.drain()
                v.tensor_sub(t1, qret(c), P1[:, 0:1024])
                v.drain()
                v.scalar_tensor_tensor(t1, t1, gcol_sb[:, c:c + 1], P0[:, 0:1024],
                                       ALU.mult, ALU.mult)
                v.drain()
                v.scalar_tensor_tensor(out2(c), t1, bcol_sb[:, c:c + 1], sg(c),
                                       ALU.add, ALU.mult).then_inc(sV.sem, 1)
                sV.inc()
                mark(sV, f"out2_{c}")
                if c == KC - 1:
                    mark(sV, "out2_done")
            VS(_)
        chk("attn", ["out2_done"],
            ([(dbg["A"][c * 128:(c + 1) * 128, :], qret(c)) for c in range(KC)]
             + [(dbg["A"][2048 + c * 128:2048 + (c + 1) * 128, :], sg(c)) for c in range(KC)]
             + [(dbg["A"][4096 + c * 128:4096 + (c + 1) * 128, :], out2(c)) for c in range(KC)]
             ) if stop == "attn" else [])

        def x1st_sec(c):
            def _(t, c=c):
                wait(t, f"x1sq{c}")
                stat_mms(t, x1bf(c), [z_raw(0), z_raw(1)][c % 2],
                         c == 0, c == KC - 1).then_inc(sT.sem, 1)
                sT.inc()
                mark(sT, f"x1st{c}")
                if c == KC - 1:
                    mark(sT, "ph7_stats")
            TS(_)

        for c in range(KC):
            if c % 2 == 0:
                def _(g, c=c):
                    attn_wblock(g, d_wo, 3, c // 2)
                GS(_)

            def _(y, c=c):
                sl = [sU0c, sU1c][c % 2]
                if c < 2:
                    wait(y, "out2_done")
                else:
                    wait(y, f"x1ev{c-2}")
                y.dma_start(out=[U0, U1][c % 2],
                            in_=d_x[c * 128:(c + 1) * 128, :]).then_inc(sl.sem, 16)
                sl.inc(16)
                mark(sl, f"xre{c}")
            YS(_)

            if c >= 2:
                x1st_sec(c - 2)

            def _(t, c=c):
                slot = (c // 2) % 2
                if c == 0:
                    wait(t, "out2_done")
                wait(t, f"aw3_f{c//2}")
                if c >= 2:
                    wait(t, f"x1ev{c-2}")
                wtile = awslot(slot).rearrange("p (ci j) -> p ci j", ci=16)
                mm_chunk(t, PP[c % 2],
                         lambda ci: wtile[:, ci, (c % 2) * 128:(c % 2) * 128 + 128],
                         lambda ci: out2(ci)).then_inc(sT.sem, 1)
                sT.inc()
                mark(sT, f"womm{c}")
                if c % 2 == 1:
                    mark(sT, f"aw3_b{c//2}")
            TS(_)

            def _(v, c=c):
                wait(v, f"womm{c}")
                wait(v, f"xre{c}")
                if c >= 2:
                    wait(v, f"x1spill{c-2}")
                x1t = [U2, U3][c % 2]
                v.scalar_tensor_tensor(x1t, PP[c % 2][:], gaf_c(c), [U0, U1][c % 2],
                                       ALU.mult, ALU.add).then_inc(sV.sem, 1)
                sV.inc()
                mark(sV, f"x1ev{c}")
                v.drain()
                v.tensor_copy(x1bf(c), x1t).then_inc(sV.sem, 1)
                sV.inc()
                mark(sV, f"x1bf{c}")
                if c >= 2:
                    wait(v, f"x1st{c-2}")
                sq = [z_raw(0), z_raw(1)][c % 2]
                v.drain()
                v.tensor_mul(sq, x1bf(c), x1bf(c)).then_inc(sV.sem, 1)
                sV.inc()
                mark(sV, f"x1sq{c}")
            VS(_)

            def _(y, c=c):
                wait(y, f"x1bf{c}")
                sp = [sSpill, sems["sSpillB"]][c % 2]
                y.dma_start(out=d_x1[c * 128:(c + 1) * 128, :], in_=[U2, U3][c % 2]
                            ).then_inc(sp.sem, 16)
                sp.inc(16)
                mark(sp, f"x1spill{c}")
            YS(_)

        x1st_sec(KC - 2)
        x1st_sec(KC - 1)

        rows_and_bcast("ph7", P2, P3, also_wait=["x1spill14", "x1spill15"])

        zsched = _fwht_schedule(KC)
        zslot = {c: c for c in range(KC)}
        zpool = [16, 17]

        def z_ap(idx):
            if idx < 16:
                return z_raw(idx)
            return [SPARE1, SPARE2][idx - 16]

        for c in range(KC):
            def _(v, c=c):
                if c == 0:
                    wait(v, "ph7_bcast")
                    wait(v, "x1spill14")
                    wait(v, "x1spill15")
                t1 = [U0, U1][c % 2]
                if c >= 2:
                    wait(v, f"h2_{c-2}")
                v.drain()
                v.tensor_sub(t1, x1bf(c), P3[:, 0:1024])
                v.drain()
                v.tensor_mul(t1, t1, P2[:, 0:1024]).then_inc(sV.sem, 1)
                sV.inc()
                mark(sV, f"xn2_{c}")
            VS(_)

            if c % 2 == 0:
                continue

            def _(s, c=c):
                wait(s, f"xn2_{c}")
                for c2 in (c - 1, c):
                    if c2 >= 2:
                        wait(s, f"h2mm{c2-2}")
                    s.activation([H2T0, H2T1][c2 % 2], [U0, U1][c2 % 2],
                                 AFT.Identity, scale=scs_c(c2), bias=sbs_c(c2)
                                 ).then_inc(sS.sem, 1)
                    sS.inc()
                    mark(sS, f"h2_{c2}")
            SS(_)

            for c2 in (c - 1, c):
                def _(t, c=c2):
                    wait(t, f"h2_{c}")
                    if c >= 2:
                        wait(t, f"zev{c-2}")
                    h2t = [H2T0, H2T1][c % 2]
                    dst = PP[c % 2]
                    li = None
                    for hf in range(2):
                        li = t.matmul(dst[:, hf * 512:(hf + 1) * 512], h128_sb[:],
                                      h2t[:, hf * 512:(hf + 1) * 512], start=True, stop=True)
                    li.then_inc(sT.sem, 1)
                    sT.inc()
                    mark(sT, f"h2mm{c}")
                TS(_)

                def _(s, c=c2):
                    wait(s, f"h2mm{c}")
                    s.copy(z_raw(c), PP[c % 2][:]).then_inc(sS.sem, 1)
                    sS.inc()
                    mark(sS, f"zev{c}")
                SS(_)

        def _(v):
            wait(v, f"h2_{KC-1}")
            dw, dr = set(), set()
            for ev in zsched:
                if ev[0] == "ready":
                    wait(v, f"zev{ev[1]}")
                else:
                    _t, st_i, a, b = ev
                    ia, ib = zslot[a], zslot[b]
                    oa = zpool.pop(0)
                    ob = zpool.pop(0)
                    if ({ia, ib} & dw) or ({oa, ob} & (dw | dr)):
                        v.drain()
                        dw.clear()
                        dr.clear()
                    v.tensor_add(z_ap(oa), z_ap(ia), z_ap(ib))
                    v.tensor_sub(z_ap(ob), z_ap(ia), z_ap(ib))
                    dw.update((oa, ob))
                    dr.update((ia, ib))
                    zpool.extend([ia, ib])
                    zslot[a], zslot[b] = oa, ob
            for c in range(KC):
                if zslot[c] >= 16:
                    dst = next(i for i in zpool if i < 16)
                    zpool.remove(dst)
                    v.drain()
                    dw.clear()
                    dr.clear()
                    v.tensor_copy(z_ap(dst), z_ap(zslot[c]))
                    dw.add(dst)
                    zpool.append(zslot[c])
                    zslot[c] = dst
            v.drain().then_inc(sV.sem, 1)
            sV.inc()
            mark(sV, "z_done")
        VS(_)
        chk("z", ["z_done"], [(dbg["B"][c * 128:(c + 1) * 128, :], z_ap(zslot[c]))
                              for c in range(KC)] if stop == "z" else [])

        usched = _fwht_schedule(JC)
        uslot = {c: c for c in range(JC)}
        upool = [66, 67, 64, 65]
        u_dw, u_dr = set(), set()
        u_forder = []

        rspare = rowmm.bitcast(BF16)
        def u_ap(idx):
            if idx < 64:
                return u_raw(idx)
            if idx < 66:
                return [SPARE1, SPARE2][idx - 64]
            return rspare[:, (idx - 66) * 1024:(idx - 65) * 1024]

        CAP = 12
        uev_groups = []
        fifo = []
        cur = None
        for ev in usched:
            if ev[0] == "ready":
                if cur is not None:
                    uev_groups.append(cur)
                cur = [ev[1], []]
            else:
                fifo.append(ev)
                continue
            while fifo and len(cur[1]) < CAP:
                cur[1].append(fifo.pop(0))
        uev_groups.append(cur)
        u_tail_ops = fifo

        for jt, ops in uev_groups:
            def _(g, jt=jt):
                for (widx, dsrc, sl) in [(0, d_wup, sWU), (1, d_wgt, sWG)]:
                    slot = jt % 2
                    if jt >= 2:
                        wait(g, f"ug_b{jt-2}")
                    else:
                        wait(g, "aw3_b7")
                    g.dma_start(out=fwslot(widx * 2 + slot),
                                in_=dsrc[jt * 128:(jt + 1) * 128, :]
                                ).then_inc(sl[slot].sem, 16)
                    sl[slot].inc(16)
                    mark(sl[slot], ["wu", "wg"][widx] + str(jt))
            GS(_)

            def _(t, jt=jt):
                slot = jt % 2
                if jt == 0:
                    wait(t, "z_done")
                wait(t, f"wu{jt}")
                wait(t, f"wg{jt}")
                if jt >= 2:
                    wait(t, f"uev{jt-2}")
                pu = PP[slot * 2]
                pg = PP[slot * 2 + 1]
                wu = fwslot(0 * 2 + slot).rearrange("p (ci j) -> p ci j", ci=16)
                wg = fwslot(1 * 2 + slot).rearrange("p (ci j) -> p ci j", ci=16)
                mm_chunk(t, pg, lambda ci: wg[:, ci, :], lambda ci: z_ap(zslot[ci]))
                mm_chunk(t, pu, lambda ci: wu[:, ci, :],
                         lambda ci: z_ap(zslot[ci])).then_inc(sT.sem, 1)
                sT.inc()
                mark(sT, f"ugmm{jt}")
                mark(sT, f"ug_b{jt}")
            TS(_)

            def _(s, jt=jt):
                wait(s, f"ugmm{jt}")
                pg = PP[(jt % 2) * 2 + 1]
                s.activation([U2, U3][jt % 2], pg[:], AFT.Silu,
                             scale=gsc_sb[:, jt:jt + 1]).then_inc(sS.sem, 1)
                sS.inc()
                mark(sS, f"sgate{jt}")
            SS(_)

            def _(v, jt=jt, ops=ops):
                wait(v, f"sgate{jt}")
                pu = PP[(jt % 2) * 2]
                v.scalar_tensor_tensor(u_raw(jt), pu[:], usc_sb[:, jt:jt + 1],
                                       [U2, U3][jt % 2], ALU.mult, ALU.mult
                                       ).then_inc(sV.sem, 1)
                sV.inc()
                mark(sV, f"uev{jt}")
                u_dw.add(jt)
                for ev in ops:
                    _t, st_i, a, b = ev
                    ia, ib = uslot[a], uslot[b]
                    oa = upool.pop(0)
                    ob = upool.pop(0)
                    if ({ia, ib} & u_dw) or ({oa, ob} & (u_dw | u_dr)):
                        v.drain()
                        u_dw.clear()
                        u_dr.clear()
                    v.tensor_add(u_ap(oa), u_ap(ia), u_ap(ib))
                    v.tensor_sub(u_ap(ob), u_ap(ia), u_ap(ib))
                    u_dw.update((oa, ob))
                    u_dr.update((ia, ib))
                    upool.extend([ia, ib])
                    uslot[a], uslot[b] = oa, ob
                    if st_i == 5:
                        v.drain().then_inc(sV.sem, 1)
                        sV.inc()
                        u_dw.clear()
                        u_dr.clear()
                        for x in (a, b):
                            if uslot[x] not in (64, 65):
                                mark(sV, f"ufin{x}")
                                u_forder.append(x)
            VS(_)

        def _(v):
            for ev in u_tail_ops:
                _t, st_i, a, b = ev
                ia, ib = uslot[a], uslot[b]
                oa = upool.pop(0)
                ob = upool.pop(0)
                if ({ia, ib} & u_dw) or ({oa, ob} & (u_dw | u_dr)):
                    v.drain()
                    u_dw.clear()
                    u_dr.clear()
                v.tensor_add(u_ap(oa), u_ap(ia), u_ap(ib))
                v.tensor_sub(u_ap(ob), u_ap(ia), u_ap(ib))
                u_dw.update((oa, ob))
                u_dr.update((ia, ib))
                upool.extend([ia, ib])
                uslot[a], uslot[b] = oa, ob
                if st_i == 5:
                    v.drain().then_inc(sV.sem, 1)
                    sV.inc()
                    u_dw.clear()
                    u_dr.clear()
                    for x in (a, b):
                        if uslot[x] not in (64, 65):
                            mark(sV, f"ufin{x}")
                            u_forder.append(x)
            for c in range(JC):
                if uslot[c] in (64, 65):
                    dst = next(i for i in upool if i < 64 or i >= 66)
                    upool.remove(dst)
                    v.drain()
                    v.tensor_copy(u_ap(dst), u_ap(uslot[c]))
                    upool.append(uslot[c])
                    uslot[c] = dst
                    v.drain().then_inc(sV.sem, 1)
                    sV.inc()
                    mark(sV, f"ufin{c}")
                    u_forder.append(c)
            v.drain().then_inc(sV.sem, 1)
            sV.inc()
            mark(sV, "u64_done")
            assert sorted(u_forder) == list(range(JC))
        VS(_)
        chk("u", ["u64_done"], [(dbg["A"][c * 128:(c + 1) * 128, :], u_ap(uslot[c]))
                                for c in range(JC)] if stop == "u" else [])

        UH_S = 16
        for ci, c in enumerate(u_forder):
            def _(t, c=c, ci=ci):
                wait(t, f"ufin{c}")
                if ci < 4:
                    wait(t, "uev63")
                else:
                    wait(t, f"uhev{u_forder[ci-4]}")
                src = u_ap(uslot[c])
                dst = PP[ci % 4]
                li = None
                for hf in range(2):
                    li = t.matmul(dst[:, hf * 512:(hf + 1) * 512], h128_sb[:],
                                  src[:, hf * 512:(hf + 1) * 512], start=True, stop=True)
                li.then_inc(sT.sem, 1)
                sT.inc()
                mark(sT, f"uhmm{c}")
            TS(_)

            if ci < UH_S:
                def _(s, c=c, ci=ci):
                    wait(s, f"uhmm{c}")
                    s.copy(u_ap(uslot[c]), PP[ci % 4][:]).then_inc(sS.sem, 1)
                    sS.inc()
                    mark(sS, f"uhev{c}")
                    if ci == UH_S - 1:
                        mark(sS, "uhS_done")
                SS(_)
            else:
                def _(v, c=c, ci=ci):
                    wait(v, f"uhmm{c}")
                    v.tensor_copy(u_ap(uslot[c]), PP[ci % 4][:]).then_inc(sV.sem, 1)
                    sV.inc()
                    mark(sV, f"uhev{c}")
                    if ci == JC - 1:
                        mark(sV, "uh_done")
                VS(_)
        chk("uh", ["uh_done"], [(dbg["A"][c * 128:(c + 1) * 128, :], u_ap(uslot[c]))
                                for c in range(JC)] if stop == "uh" else [])

        for dt_i in range(KC):
            def _(g, dt_i=dt_i):
                slot = dt_i % 2
                if dt_i == 0:
                    wait(g, "u64_done")
                if dt_i >= 2:
                    wait(g, f"dn_b{dt_i-2}")
                g.dma_start(out=dwslot(slot),
                            in_=d_wdn[dt_i * 128:(dt_i + 1) * 128, :]
                            ).then_inc(sWD[slot].sem, 16)
                sWD[slot].inc(16)
                mark(sWD[slot], f"wdn{dt_i}")
            GS(_)

            def _(y, dt_i=dt_i):
                sl = [sU0c, sU1c][dt_i % 2]
                if dt_i == 0:
                    wait(y, "x1spill14")
                    wait(y, "x1spill15")
                    wait(y, "u64_done")
                if dt_i >= 2:
                    wait(y, f"outev{dt_i-2}")
                y.dma_start(out=[U0, U1][dt_i % 2],
                            in_=d_x1[dt_i * 128:(dt_i + 1) * 128, :]).then_inc(sl.sem, 16)
                sl.inc(16)
                mark(sl, f"x1re{dt_i}")
            YS(_)

            def _(t, dt_i=dt_i):
                if dt_i == 0:
                    wait(t, "uh_done")
                    wait(t, "uhS_done")
                wait(t, f"wdn{dt_i}")
                if dt_i >= 2:
                    wait(t, f"outev{dt_i-2}")
                dst = PP[dt_i % 2]
                wtile = dwslot(dt_i % 2).rearrange("p (c j) -> p c j", c=64)
                li = None
                for c in range(JC):
                    src = u_ap(uslot[c])
                    for hf in range(2):
                        li = t.matmul(dst[:, hf * 512:(hf + 1) * 512], wtile[:, c, :],
                                      src[:, hf * 512:(hf + 1) * 512],
                                      start=(c == 0), stop=(c == JC - 1))
                li.then_inc(sT.sem, 1)
                sT.inc()
                mark(sT, f"dnmm{dt_i}")
                mark(sT, f"dn_b{dt_i}")
            TS(_)

            def _(v, dt_i=dt_i):
                wait(v, f"dnmm{dt_i}")
                wait(v, f"x1re{dt_i}")
                if dt_i >= 2:
                    wait(v, f"outst{dt_i-2}")
                v.scalar_tensor_tensor([U2, U3][dt_i % 2], PP[dt_i % 2][:],
                                       gwd_sb[:, dt_i:dt_i + 1], [U0, U1][dt_i % 2],
                                       ALU.mult, ALU.add).then_inc(sV.sem, 1)
                sV.inc()
                mark(sV, f"outev{dt_i}")
            VS(_)

            def _(y, dt_i=dt_i):
                wait(y, f"outev{dt_i}")
                so = sOutP[dt_i % 2]
                y.dma_start(out=d_out[dt_i * 128:(dt_i + 1) * 128, :],
                            in_=[U2, U3][dt_i % 2]).then_inc(so.sem, 16)
                so.inc(16)
                mark(so, f"outst{dt_i}")
            YS(_)


    try:
        _phases()
    except _StopEmit:
        pass

    def _(y):
        y.wait_ge(sems["sOut"].sem, sems["sOut"].v)
        y.wait_ge(sems["sOutB"].sem, sems["sOutB"].v)
    YS(_)

    es.close()
    return nc


_NC_CACHE = {}


def _get_nc(debug=False):
    key = bool(debug)
    if key not in _NC_CACHE:
        _NC_CACHE[key] = build_nc(debug)
    return _NC_CACHE[key]


def _hadamard128():
    h = np.array([[1.0]], dtype=np.float32)
    while h.shape[0] < 128:
        h = np.block([[h, h], [h, -h]])
    return h


def make_in_maps(inputs):
    bf = ml_dtypes.bfloat16
    x = np.asarray(inputs["L0_attn_x"], dtype=np.float32)
    x_fm = np.ascontiguousarray(x.reshape(8192, 2048).T)
    wmod = np.asarray(inputs["L0_wmod"], dtype=np.float32)
    cvec = np.asarray(inputs["L0_c"], dtype=np.float32).reshape(2048)
    cols16 = lambda a: np.ascontiguousarray(np.asarray(a, dtype=np.float32).reshape(16, 128).T)
    cols64 = lambda a: np.ascontiguousarray(np.asarray(a, dtype=np.float32).reshape(64, 128).T)
    wsc = np.asarray(inputs["L0_ffn_Wscale_ug"], dtype=np.float32)
    def tile_w(wt, ncol):
        K, Mfull = wt.shape
        nc_, nb = K // 128, Mfull // ncol
        return np.ascontiguousarray(
            wt.reshape(nc_, 128, nb, ncol).transpose(2, 1, 0, 3).reshape(nb * 128, nc_ * ncol))

    shared = {
        "wqT": tile_w(np.asarray(inputs["L0_attn_Wq"], dtype=np.float32).T, 256),
        "wvT": tile_w(np.asarray(inputs["L0_attn_Wv"], dtype=np.float32).T, 256),
        "wgT": tile_w(np.asarray(inputs["L0_attn_Wg"], dtype=np.float32).T, 256),
        "woT": tile_w(np.asarray(inputs["L0_attn_Wo"], dtype=np.float32).T, 256),
        "wupT": tile_w(np.asarray(inputs["L0_ffn_W_up"], dtype=np.float32).T, 128),
        "wgtT": tile_w(np.asarray(inputs["L0_ffn_W_gate"], dtype=np.float32).T, 128),
        "wdnT": tile_w(np.asarray(inputs["L0_ffn_W_down"], dtype=np.float32).T, 128),
        "c_cols": cols16(cvec), "beta_row": np.asarray(inputs["L0_attn_beta"], dtype=np.float32).reshape(1, 16),
        "g_cols": cols16(inputs["L0_attn_ln_gamma"]),
        "b_cols": cols16(inputs["L0_attn_ln_beta"]),
        "su_cols": cols16(inputs["L0_ffn_SU_ug"]),
        "wsuu_cols": cols64(wsc[:8192]), "wsug_cols": cols64(wsc[8192:]),
        "sudp_cols": cols64(inputs["L0_ffn_SU_dp"]),
        "wsdp_cols": cols16(inputs["L0_ffn_Wscale_dp"]),
        "h128": _hadamard128().astype(bf), "ident_bf": np.eye(128, dtype=bf),
        "ident_f32": np.eye(128, dtype=np.float32),
    }
    in_maps = []
    for k in range(NCORES):
        wexp = np.zeros((8, 16), dtype=np.float32)
        wmask = np.zeros((8, 16), dtype=np.float32)
        for j in range(NCORES):
            if j < k:
                wexp[j, :] = 1024.0 * (k - 1 - j)
                wmask[j, :] = 1.0
        m = dict(shared)
        m["x_fm"] = np.ascontiguousarray(x_fm[:, k * T:(k + 1) * T])
        m["wmod_sl"] = tile_w(np.ascontiguousarray(wmod[:, k * MODW:(k + 1) * MODW]), 128)
        m["wexp_row"] = wexp
        m["wmask_row"] = wmask
        in_maps.append(m)
    return in_maps


def kernel(**inputs):
    in_maps = make_in_maps(inputs)
    nc = _get_nc()
    res = run_bass_kernel_spmd(nc, in_maps, core_ids=list(range(NCORES)))
    out_fm = np.concatenate([res.results[k]["OUT"] for k in range(NCORES)], axis=1)
    return np.ascontiguousarray(out_fm.T).reshape(1, 8192, 2048)
